# version 04
# baseline (speedup 1.0000x reference)
import numpy as np
import ml_dtypes
import concourse.bass as bass
import concourse.mybir as mybir
from concourse.bass_utils import run_bass_kernel_spmd

F32 = mybir.dt.float32
BF16 = mybir.dt.bfloat16
I32 = mybir.dt.int32
AF = mybir.ActivationFunctionType
ALU = mybir.AluOpType
AX = mybir.AxisListType.X
BF16NP = ml_dtypes.bfloat16


def full_cfg():
    return dict(n_cores=8, Dm=4096, F=11008, T_own=1024, HALO=512, TB=512)



class SemT:
    __slots__ = ("name", "cnt", "h")

    def __init__(self, name):
        self.name, self.cnt, self.h = name, 0, None

    def take(self, n=1):
        self.cnt += n
        return self.cnt


class Rec:
    __slots__ = ("fn", "incs", "tagval")

    def __init__(self, fn):
        self.fn, self.incs, self.tagval = fn, [], None


class Buf:
    __slots__ = ("ap", "w", "readers", "name")

    def __init__(self, ap, name=""):
        self.ap, self.w, self.readers, self.name = ap, None, {}, name


class DBuf:

    def __init__(self, ap, shape, rgran, cgran, name=""):
        self.ap = ap
        self.shape = shape
        self.rg, self.cg = rgran, cgran
        self.name = name
        self.gr = {}

    def _g(self, ri, ci):
        key = (ri, ci)
        b = self.gr.get(key)
        if b is None:
            b = Buf(None, f"{self.name}[{ri},{ci}]")
            self.gr[key] = b
        return b

    def rng(self, r0, r1, c0=0, c1=None):
        if c1 is None:
            c1 = self.shape[1]
        out = []
        for ri in range(r0 // self.rg, (r1 - 1) // self.rg + 1):
            for ci in range(c0 // self.cg, (c1 - 1) // self.cg + 1):
                out.append(self._g(ri, ci))
        return out


QS = ("t", "v", "s", "g", "y")


class Gen:
    __slots__ = ("sem", "end")

    def __init__(self, sem):
        self.sem, self.end = sem, None


class Plan:
    DMA_POOL = 6

    def __init__(self):
        self.q = {k: [] for k in QS}
        self.sems = []
        self.pending = {k: [] for k in QS}
        self.chain = {k: self.new_sem("chain_" + k) for k in ("t", "v", "s", "g")}
        self.waited = {k: {} for k in QS}
        self.dpool = {k: [self.new_sem(f"dma_{k}{i}") for i in range(self.DMA_POOL)]
                      for k in ("y", "s", "g", "v")}
        self.didx = {k: 0 for k in self.dpool}
        self.dopen = {k: None for k in self.dpool}

    def new_sem(self, name):
        s = SemT(name)
        self.sems.append(s)
        return s

    def raw(self, qn, fn):
        rec = Rec(fn)
        self.q[qn].append(rec)
        return rec

    def wait(self, qn, sem, val):
        if val <= 0:
            return
        w = self.waited[qn]
        if w.get(sem.name, 0) >= val:
            return
        w[sem.name] = val
        self.raw(qn, lambda E, s=sem, v=val: E.wait_ge(s.h, v))

    def _resolve(self, dep):
        kind = dep[0]
        if kind == "s":
            return dep[1], dep[2]
        if kind == "d":
            qn, gen = dep[1], dep[2]
            if gen.end is None:
                gen.end = gen.sem.cnt
                if self.dopen[qn] is gen:
                    self.dopen[qn] = None
            return gen.sem, gen.end
        qn, rec = dep[1], dep[2]
        if rec.tagval is None:
            pend = self.pending[qn]
            idx = pend.index(rec)
            ch = self.chain[qn]
            ch.take(idx + 1)
            rec.incs.append((ch, idx + 1))
            for r in pend[: idx + 1]:
                r.tagval = ch.cnt
            del pend[: idx + 1]
        return self.chain[qn], rec.tagval

    def dep_wait(self, qn, dep):
        sem, val = self._resolve(dep)
        self.wait(qn, sem, val)

    def _pre(self, qn, reads, writes, waw_free=False):
        for b in reads:
            if b.w is not None:
                self.dep_wait(qn, b.w)
        for b in writes:
            if not waw_free and b.w is not None:
                self.dep_wait(qn, b.w)
            if not waw_free:
                for d in list(b.readers.values()):
                    self.dep_wait(qn, d)

    def cop(self, qn, fn, reads=(), writes=(), waw_free=False):
        self._pre(qn, reads, writes, waw_free)
        rec = self.raw(qn, fn)
        self.pending[qn].append(rec)
        dep = ("c", qn, rec)
        for b in reads:
            b.readers[qn] = dep
        for b in writes:
            b.w = dep
            b.readers = {}
        return rec

    def dma(self, qn, fn, reads=(), writes=()):
        self._pre(qn, reads, writes)
        gen = self.dopen[qn]
        if gen is None:
            pool = self.dpool[qn]
            sem = pool[self.didx[qn] % len(pool)]
            self.didx[qn] += 1
            if sem.cnt > 0:
                self.wait(qn, sem, sem.cnt)
            gen = Gen(sem)
            self.dopen[qn] = gen
        gen.sem.take(16)
        rec = self.raw(qn, fn)
        rec.incs.append((gen.sem, 16))
        dep = ("d", qn, gen)
        for b in reads:
            b.readers[qn + "dma"] = dep
        for b in writes:
            b.w = dep
            b.readers = {}
        return dep

    def flush_pending(self):
        for qn in ("t", "v", "s", "g"):
            pend = self.pending[qn]
            if pend:
                self._resolve(("c", qn, pend[-1]))

    def barrier(self):
        self.flush_pending()
        for pq in self.dpool:
            g = self.dopen[pq]
            if g is not None:
                g.end = g.sem.cnt
                self.dopen[pq] = None
        targets = [(ch, ch.cnt) for ch in self.chain.values() if ch.cnt]
        for pool in self.dpool.values():
            targets.extend((s, s.cnt) for s in pool if s.cnt)
        for qn in QS:
            for sem, val in targets:
                self.wait(qn, sem, val)

    def finish(self, qn="y"):
        for pq, pool in self.dpool.items():
            g = self.dopen[pq]
            if g is not None:
                g.end = g.sem.cnt
                self.dopen[pq] = None
            for s in pool:
                self.wait(qn, s, s.cnt)
        for k, ch in self.chain.items():
            self.wait(qn, ch, ch.cnt)


def emit_all(nc, plan):
    import contextlib
    stack = contextlib.ExitStack()
    for s in plan.sems:
        s.h = stack.enter_context(nc.semaphore(s.name))
    block = stack.enter_context(nc.Block())

    def runq(E, qn):
        for rec in plan.q[qn]:
            ins = rec.fn(E)
            for sem, n in rec.incs:
                ins.then_inc(sem.h, n)

    @block.sync
    def _(E):
        runq(E, "y")

    @block.vector
    def _(E):
        runq(E, "v")

    @block.scalar
    def _(E):
        runq(E, "s")

    @block.gpsimd
    def _(E):
        runq(E, "g")

    @block.tensor
    def _(E):
        runq(E, "t")

    return stack



RAMP_NAMES = ["r4c", "k4c", "rL4", "kL4", "r8", "k8", "r16", "k16",
              "r32", "k32", "r64", "k64", "ri", "ku"]
BIGR = 1.0e4
CHUNK = 128


def make_ramps():
    i = np.arange(CHUNK).astype(np.float64)
    Z = lambda cond, g: np.where(cond, g, BIGR)
    g = {
        "r4c": (i % 4) - 2.0,
        "k4c": 1.0 - (i % 4),
        "rL4": Z(i % 8 >= 4, (i % 8) - 4.0),
        "kL4": Z(i % 8 < 4, 3.0 - (i % 8)),
        "r8":  Z(i % 16 >= 8, (i % 16) - 8.0),
        "k8":  Z(i % 16 < 8, 7.0 - (i % 16)),
        "r16": Z(i % 32 >= 16, (i % 32) - 16.0),
        "k16": Z(i % 32 < 16, 15.0 - (i % 32)),
        "r32": Z(i % 64 >= 32, (i % 64) - 32.0),
        "k32": Z(i % 64 < 32, 31.0 - (i % 64)),
        "r64": Z(i % 128 >= 64, (i % 128) - 64.0),
        "k64": Z(i % 128 < 64, 63.0 - (i % 128)),
        "ri":  i,
        "ku":  (CHUNK - 1.0) - i,
    }
    return np.stack([g[n] for n in RAMP_NAMES]).astype(np.float32)


MASK_NAMES = ["m4", "m8", "m16", "mI", "mB"]


def make_masks():
    jj = np.arange(CHUNK)[:, None]
    ii = np.arange(CHUNK)[None, :]
    m4 = ((jj // 4 == ii // 4) & (jj < ii)).astype(np.float32)
    m8 = (jj // 8 == ii // 8).astype(np.float32)
    m16 = (jj // 16 == ii // 16).astype(np.float32)
    mI = (jj == ii).astype(np.float32)
    mB = np.zeros((CHUNK, CHUNK))
    for L in (16, 32, 64):
        W = 2 * L
        mB += ((jj // W == ii // W) & (jj % W < L) & (ii % W >= L))
    mB = (mB > 0).astype(np.float32)
    return np.stack([m4, m8, m16, mI, mB]).astype(BF16NP)



def build_program(cfg):
    Dm, F, T_own, HALO, TB = (cfg["Dm"], cfg["F"], cfg["T_own"], cfg["HALO"],
                              cfg["TB"])
    T_all = T_own + HALO
    H = Dm // 64
    P2 = H // 2
    NG = (P2 + 1) // 2
    ND = Dm // 128
    OBW = min(512, Dm)
    NO = Dm // OBW
    C = CHUNK
    NCH = T_all // C
    NCH_H = HALO // C
    NB = T_all // TB
    NT = TB // 128
    NQ1 = Dm // 128
    NQ2 = F // 128
    OB2 = min(256, Dm)
    NO2 = Dm // OB2
    TB2 = min(512, T_own)
    NFB = T_own // TB2
    assert HALO % TB == 0 and T_all % TB == 0 and TB % 128 == 0
    assert T_own % C == 0 and HALO % C == 0 and P2 % 2 == 0

    nc = bass.Bass("TRN2", target_bir_lowering=False, debug=False)

    def ein(name, shape, dt):
        return nc.dram_tensor(name, list(shape), dt, kind="ExternalInput").ap()

    x_in = ein("x_c", [T_all, Dm], F32)
    wT = {w: ein(w + "T", [Dm, Dm], F32) for w in ("wk", "wv", "wr", "wg", "wo")}
    mix_in = {m: ein(m, [128, ND], F32) for m in ("mk", "mv", "mr", "mg")}
    td_in = ein("td_pair", [128, P2], F32)
    tf_in = ein("tf_pair", [128, P2], F32)
    gnw_in = ein("gnw", [1, Dm], F32)
    gnb_in = ein("gnb", [1, Dm], F32)
    ramps_in = ein("ramps", [len(RAMP_NAMES), 128], F32)
    masks_in = ein("masks", [len(MASK_NAMES), 128, 128], BF16)
    w1p = ein("w1p", [64, NQ1, F], I32)
    w3p = ein("w3p", [64, NQ1, F], I32)
    w2p = ein("w2p", [64, NQ2, Dm], I32)
    s1d = ein("s1d", [NQ1, 2, F], F32)
    s3d = ein("s3d", [NQ1, 2, F], F32)
    s2d = ein("s2d", [NQ2, 2, Dm], F32)
    out_c = nc.dram_tensor("out_c", [T_own, Dm], F32, kind="ExternalOutput").ap()

    def dscr(name, shape, dt):
        return nc.dram_tensor(name, list(shape), dt, kind="Internal").ap()

    def dbuf(name, shape, dt, rgran, cgran):
        return DBuf(dscr(name, shape, dt), shape, rgran, cgran, name)

    ln1_hbm = dbuf("ln1_hbm", [T_all + 128, Dm], BF16, 128, Dm)
    k_hbm = dbuf("k_hbm", [T_all, Dm], BF16, 128, OBW)
    v_hbm = dbuf("v_hbm", [T_all, Dm], BF16, 128, OBW)
    r_hbm = dbuf("r_hbm", [T_own, Dm], BF16, 128, OBW)
    g_hbm = dbuf("g_hbm", [T_own, Dm], BF16, 128, OBW)
    xn_hbm = dbuf("xn_hbm", [T_own, Dm], F32, 128, 128)
    y_hbm = dbuf("y_hbm", [T_own, Dm], BF16, 128, Dm)
    x2_hbm = dbuf("x2_hbm", [T_own, Dm], F32, 128, OBW)
    ln2_hbm = dbuf("ln2_hbm", [T_own, Dm], BF16, 128, Dm)
    h_hbm = dbuf("h_hbm", [T_own, F], BF16, 128, 512)
    wbf = {w: dbuf(w + "bf", [Dm, Dm], BF16, Dm, Dm)
           for w in ("wk", "wv", "wr", "wg", "wo")}
    w1bf = dbuf("w1bf", [Dm, F], BF16, 64, 512)
    w3bf = dbuf("w3bf", [Dm, F], BF16, 64, 512)
    w2bf = dbuf("w2bf", [F, Dm], BF16, 64, 512)

    PH = cfg.get("phases", {"cast", "deq", "bln", "bproj", "wkv", "wkv2",
                            "wkv3", "wkv4", "wkvA4", "wkvAE", "wkvEv", "gn",
                            "op", "ln2", "ffa", "ffw2"})
    p = Plan()
    import contextlib
    g_es = contextlib.ExitStack()

    def mk_sbuf(stk):
        def sbuf(name, shape, dt):
            return stk.enter_context(nc.sbuf_tensor(name, list(shape), dt))

        def B(name, shape, dt):
            return Buf(sbuf(name, shape, dt)[:], name)
        return sbuf, B

    sbufG, BG = mk_sbuf(g_es)

    mix_sb = {m: BG(m + "_sb", [128, ND], F32) for m in mix_in}
    mix1_sb = {m: BG(m + "1_sb", [128, ND], F32) for m in mix_in}
    tdp_sb = BG("tdp_sb", [128, P2], F32)
    tfp_sb = BG("tfp_sb", [128, P2], F32)
    E_sb = BG("E_sb", [128, P2], F32)
    tdC_sb = BG("tdC_sb", [128, P2], F32)
    ramps_sb = BG("ramps_sb", [128, len(RAMP_NAMES) * 128], F32)
    masks_sb = BG("masks_sb", [128, len(MASK_NAMES) * 128], BF16)
    eps_sb = BG("eps_sb", [128, 1], F32)

    def ramp(nm):
        i = RAMP_NAMES.index(nm)
        return ramps_sb.ap[:, 128 * i:128 * (i + 1)]

    def maskv(nm, nh):
        i = MASK_NAMES.index(nm)
        one = masks_sb.ap[:, 128 * i:128 * (i + 1)]
        return one.unsqueeze(1).to_broadcast([128, nh, 128])

    PS = [Buf(nc.alloc_psum_tensor(f"P{i}", [128, 512], F32)[:], f"P{i}")
          for i in range(7)]
    PT = Buf(nc.alloc_psum_tensor("PT", [128, 1024], BF16)[:], "PT")

    for m in mix_in:
        p.dma("y", lambda E, a=mix_sb[m].ap, b=mix_in[m]: E.dma_start(out=a, in_=b),
              writes=(mix_sb[m],))
    p.dma("y", lambda E: E.dma_start(out=tdp_sb.ap, in_=td_in), writes=(tdp_sb,))
    p.dma("y", lambda E: E.dma_start(out=tfp_sb.ap, in_=tf_in), writes=(tfp_sb,))
    p.dma("y", lambda E: E.dma_start(
        out=ramps_sb.ap.rearrange("p (n c) -> p n c", n=len(RAMP_NAMES)),
        in_=ramps_in.unsqueeze(0).to_broadcast(
            [128, len(RAMP_NAMES), 128])), writes=(ramps_sb,))
    p.dma("y", lambda E: E.dma_start(
        out=masks_sb.ap.rearrange("p (n c) -> p n c", n=len(MASK_NAMES)),
        in_=masks_in.transpose([1, 0, 2])), writes=(masks_sb,))
    for m in mix_in:
        p.cop("v", lambda E, o=mix1_sb[m].ap, i=mix_sb[m].ap: E.tensor_scalar(
            o, i, 1.0, -1.0, ALU.subtract, ALU.mult),
            reads=(mix_sb[m],), writes=(mix1_sb[m],))
    p.cop("v", lambda E: E.memset(eps_sb.ap, 1e-5), writes=(eps_sb,))
    p.cop("s", lambda E: E.activation(E_sb.ap, tdp_sb.ap, AF.Exp),
          reads=(tdp_sb,), writes=(E_sb,))
    p.cop("s", lambda E: E.activation(tdC_sb.ap, E_sb.ap, AF.Exp, scale=-float(C)),
          reads=(E_sb,), writes=(tdC_sb,))

    for w in ("wk", "wv", "wr", "wg", "wo") if "cast" in PH else ():
        p.dma("g", lambda E, o=wbf[w].ap, i=wT[w]: E.dma_start(out=o, in_=i),
              writes=wbf[w].rng(0, Dm))

    inv_d = 1.0 / Dm

    def emit_ln(src_ap, src_row, dst_hbm, dst_row, bufs, src_d=None, n=None):
        xa, st, lst, sq = bufs
        rd = () if src_d is None else tuple(src_d.rng(src_row, src_row + 128))
        p.dma("y", lambda E, o=xa.ap, i=src_ap, t=src_row: E.dma_start(
            out=o, in_=i[t:t + 128, :]), reads=rd, writes=(xa,))
        sums, sumsq = st.ap[:, 0:1], st.ap[:, 1:2]
        mean, varv = st.ap[:, 2:3], st.ap[:, 3:4]
        sd, rstd, m2 = st.ap[:, 4:5], st.ap[:, 5:6], st.ap[:, 6:7]
        p.cop("v", lambda E, o=sums, i=xa.ap: E.reduce_sum(o, i, AX),
              reads=(xa,), writes=(st,))
        p.cop("s", lambda E, o=sq.ap, i=xa.ap, a=sumsq: E.activation(
            o, i, AF.Square, accum_out=a), reads=(xa,), writes=(sq, st))
        p.cop("v", lambda E, o=mean, i=sums: E.tensor_single_scalar(
            o, i, inv_d, ALU.mult), reads=(st,), writes=(st,))
        p.cop("v", lambda E, o=m2, i=mean: E.tensor_tensor(
            o, i, i, ALU.mult), reads=(st,), writes=(st,))
        p.cop("v", lambda E, o=varv, i=sumsq: E.tensor_single_scalar(
            o, i, inv_d, ALU.mult), reads=(st,), writes=(st,))
        p.cop("v", lambda E, o=varv, a=varv, b=m2: E.tensor_sub(o, a, b),
              reads=(st,), writes=(st,))
        p.cop("s", lambda E, o=sd, i=varv: E.activation(
            o, i, AF.Sqrt, bias=eps_sb.ap), reads=(st, eps_sb), writes=(st,))
        p.cop("v", lambda E, o=rstd, i=sd: E.reciprocal(o, i),
              reads=(st,), writes=(st,))
        p.cop("v", lambda E, o=lst.ap, i=xa.ap, a=mean, b=rstd: E.tensor_scalar(
            o, i, a, b, ALU.subtract, ALU.mult), reads=(xa, st), writes=(lst,))
        p.dma("y", lambda E, o=dst_hbm.ap, r=dst_row, i=lst.ap: E.dma_start(
            out=o[r:r + 128, :], in_=i), reads=(lst,),
            writes=dst_hbm.rng(dst_row, dst_row + 128))

    with contextlib.ExitStack() as ph:
        sbufP, BP = mk_sbuf(ph)
        xa_sb = [BP(f"xa{i}", [128, Dm], F32) for i in range(2)]
        sq_sb = [BP(f"sq{i}", [128, Dm], F32) for i in range(2)]
        stat_sb = [BP(f"stt{i}", [128, 8], F32) for i in range(2)]
        ln_st = [BP(f"lnst{i}", [128, Dm], BF16) for i in range(2)]
        p.cop("v", lambda E: E.memset(ln_st[0].ap, 0.0), writes=(ln_st[0],))
        p.dma("y", lambda E: E.dma_start(out=ln1_hbm.ap[0:128, :],
                                         in_=ln_st[0].ap),
              reads=(ln_st[0],), writes=ln1_hbm.rng(0, 128))
        for tt in range(T_all // 128) if "bln" in PH else ():
            par = tt % 2
            emit_ln(x_in, tt * 128, ln1_hbm, 128 + tt * 128,
                    (xa_sb[par], stat_sb[par], ln_st[par], sq_sb[par]))
        p.barrier()

    with contextlib.ExitStack() as ph:
        sbufP, BP = mk_sbuf(ph)
        MH = 512
        dq_pk = [BP(f"dq_pk{i}", [128, MH], I32) for i in range(2)]
        dq_ms = BP("dq_ms", [128, MH], I32)
        dq_ls = BP("dq_ls", [128, MH], I32)
        dq_sc = [BP(f"dq_sc{i}", [128, 2 * MH], F32) for i in range(2)]
        dq_o = [BP(f"dq_o{i}", [128, MH], BF16) for i in range(2)]

        def dq_chunk(pk_in, s_in, out_hbm, qp, hs, WH, par):
            pkb, scb = dq_pk[par], dq_sc[par]
            pk = pkb.ap[:, 0:WH]
            for half in range(2):
                p.dma("g", lambda E, o=pk, i=pk_in, q=qp, h=hs, hf=half,
                      W_=WH: E.dma_start(
                    out=o[64 * hf:64 * hf + 64, :],
                    in_=i[:, 2 * q + hf, h:h + W_]),
                    writes=(pkb,))
            sc = scb.ap[:, 0:2 * WH].rearrange("p (b c) -> p b c", b=2)
            for half in range(2):
                p.dma("g", lambda E, o=sc[64 * half:64 * half + 64],
                      i=s_in, q=qp, h=hs, hf=half, W_=WH: E.dma_start(
                    out=o,
                    in_=i[2 * q + hf:2 * q + hf + 1, :, h:h + W_]
                        .to_broadcast([64, 2, W_])),
                    writes=(scb,))
            ms, ls = dq_ms.ap[:, 0:WH], dq_ls.ap[:, 0:WH]
            p.cop("v", lambda E, o=ms, i=pk: E.tensor_single_scalar(
                o, i, 4, ALU.arith_shift_right),
                reads=(pkb,), writes=(dq_ms,))
            p.cop("v", lambda E, o=ls, i=pk: E.tensor_scalar(
                o, i, 15, 8, ALU.bitwise_and, ALU.bitwise_xor),
                reads=(pkb,), writes=(dq_ls,))
            o0, o1 = dq_o[0].ap[:, 0:WH], dq_o[1].ap[:, 0:WH]
            p.cop("v", lambda E, o=o0, i=ms, s=sc[:, 0]:
                  E.scalar_tensor_tensor(o, i, 0.0, s, ALU.add, ALU.mult),
                  reads=(dq_ms, scb), writes=(dq_o[0],))
            p.cop("v", lambda E, o=o1, i=ls, s=sc[:, 1]:
                  E.scalar_tensor_tensor(o, i, -8.0, s, ALU.add, ALU.mult),
                  reads=(dq_ls, scb), writes=(dq_o[1],))
            for half, bsrc in ((0, dq_o[0]), (1, dq_o[1])):
                for sub in range(2):
                    r0 = 128 * (2 * qp + sub) + 64 * half
                    p.dma("g", lambda E, b=bsrc, r=r0, h=hs, sb_=sub,
                          W_=WH: E.dma_start(
                        out=out_hbm.ap[r:r + 64, h:h + W_],
                        in_=b.ap[64 * sb_:64 * sb_ + 64, 0:W_]),
                        reads=(bsrc,),
                        writes=out_hbm.rng(r0, r0 + 64, hs, hs + WH))

        dq_tasks = []
        if "deq" in PH:
            for pk_in, s_in, out_hbm, NQ, W in (
                    (w1p, s1d, w1bf, NQ1, F), (w3p, s3d, w3bf, NQ1, F),
                    (w2p, s2d, w2bf, NQ2, Dm)):
                nchunks = (W + MH - 1) // MH
                for qp in range(NQ // 2):
                    for hc in range(nchunks):
                        hs = hc * MH
                        dq_tasks.append((pk_in, s_in, out_hbm, qp, hs,
                                         min(MH, W - hs)))
        dq_pos = [0]

        def dq_drain(n):
            while n > 0 and dq_pos[0] < len(dq_tasks):
                t = dq_tasks[dq_pos[0]]
                dq_chunk(*t, dq_pos[0] % 2)
                dq_pos[0] += 1
                n -= 1

        lnT_t = sbufP("lnT_sb", [128, ND * (TB + 128)], BF16)
        lnT_v = lnT_t[:].rearrange("p (n c) -> p n c", n=ND)
        lnT_sb = [Buf(lnT_v[:, d, :], f"lnT{d}") for d in range(ND)]
        mixT_t = [sbufP(f"mixT{i}", [128, ND * TB], BF16) for i in range(2)]
        mixT_sb = [[Buf(mixT_t[i][:].rearrange("p (n c) -> p n c", n=ND)[:, d, :],
                        f"mixT{i}_{d}") for d in range(ND)] for i in range(2)]
        mixtmp = [BP(f"mixtmp{i}", [128, TB], BF16) for i in range(2)]
        wslab_t = [sbufP(f"wslab{i}", [128, ND * OBW], BF16) for i in range(2)]
        wslab_sb = [Buf(wslab_t[i][:].rearrange("p (n c) -> p n c", n=ND),
                        f"wsl{i}") for i in range(2)]
        ev_sb = [BP(f"ev{i}", [128, OBW], BF16) for i in range(2)]

        def fetch_lnT(t0):
            for d in range(ND):
                p.dma("s", lambda E, o=lnT_sb[d].ap, dd=d, t=t0:
                      E.dma_start_transpose(
                    o, ln1_hbm.ap[t:t + TB + 128, 128 * dd:128 * dd + 128]),
                    reads=tuple(ln1_hbm.rng(t0, t0 + TB + 128)),
                    writes=(lnT_sb[d],))

        def build_mixT(mname, parity):
            mx = mixT_sb[parity]
            for d in range(ND):
                cur = lnT_sb[d].ap[:, 128:128 + TB]
                sh = lnT_sb[d].ap[:, 127:127 + TB]
                tmp = mixtmp[d % 2]
                p.cop("v", lambda E, o=tmp.ap[:, 0:TB], i=sh,
                      s=mix1_sb[mname].ap[:, d:d + 1]: E.tensor_scalar(
                      o, i, s, None, ALU.mult),
                      reads=(lnT_sb[d], mix1_sb[mname]), writes=(tmp,))
                p.cop("v", lambda E, o=mx[d].ap, i=cur,
                      s=mix_sb[mname].ap[:, d:d + 1], t=tmp.ap[:, 0:TB]:
                      E.scalar_tensor_tensor(o, i, s, t, ALU.mult, ALU.add),
                      reads=(lnT_sb[d], mix_sb[mname], tmp), writes=(mx[d],))
            return mx

        def mm_sweep(stat_of, slab, bank, nk, obw, tpos=None):
            for kk in range(nk):
                st_buf, st_ap = stat_of(kk)
                sl_ap = slab.ap[:, kk, 0:obw]
                if kk == 0:
                    p.cop("t", lambda E, o=bank.ap[:, 0:obw], a=st_ap, b=sl_ap:
                          E.matmul(o, a, b, start=True, stop=(nk == 1)),
                        reads=(st_buf, slab), writes=(bank,))
                else:
                    p.cop("t", lambda E, o=bank.ap[:, 0:obw], a=st_ap, b=sl_ap,
                          sp=(kk == nk - 1): E.matmul(o, a, b, start=False,
                                                      stop=sp),
                        reads=(st_buf, slab), writes=(bank,), waw_free=True)

        PROJ = [("wk", "mk", k_hbm, False), ("wv", "mv", v_hbm, False),
                ("wr", "mr", r_hbm, True), ("wg", "mg", g_hbm, True)]
        for blk in range(NB) if "bproj" in PH else ():
            t0 = blk * TB
            fetch_lnT(t0)
            for pi, (w, m, dst, own) in enumerate(PROJ):
                if own and blk < HALO // TB:
                    continue
                mx = build_mixT(m, pi % 2)
                for ob in range(NO):
                    slab = wslab_sb[ob % 2]
                    p.dma("y", lambda E, o=slab.ap[:, :, 0:OBW], i=wbf[w].ap,
                          b=ob: E.dma_start(
                        out=o, in_=i.rearrange("(n p) c -> p n c", p=128)
                        [:, :, OBW * b:OBW * b + OBW]),
                        reads=tuple(wbf[w].rng(0, Dm)), writes=(slab,))
                    for ts in range(NT):
                        bank = PS[ts % 2]
                        mm_sweep(lambda kk: (mx[kk],
                                             mx[kk].ap[:, 128 * ts:128 * ts + 128]),
                                 slab, bank, ND, OBW)
                        evb = ev_sb[ts % 2]
                        fn = AF.Silu if w == "wg" else AF.Copy
                        p.cop("s", lambda E, o=evb.ap[:, 0:OBW],
                              i=bank.ap[:, 0:OBW], f=fn: E.activation(o, i, f),
                              reads=(bank,), writes=(evb,))
                        dst_t = t0 + ts * 128 - (HALO if own else 0)
                        p.dma("y", lambda E, o=dst.ap, t=dst_t, b=ob,
                              i=evb.ap[:, 0:OBW]: E.dma_start(
                            out=o[t:t + 128, OBW * b:OBW * b + OBW], in_=i),
                            reads=(evb,),
                            writes=dst.rng(dst_t, dst_t + 128, ob * OBW,
                                           ob * OBW + OBW))
                    dq_drain(13)
        dq_drain(10 ** 9)
        p.barrier()

    NMAT = len(RAMP_NAMES)
    KKINDS = ["k4c", "kL4", "k8", "k16", "k32", "k64"]
    RKINDS = ["r4c", "rL4", "r8", "r16", "r32", "r64"]
    NKK = len(KKINDS)
    NRK = len(RKINDS)
    import contextlib as _ctx
    with _ctx.ExitStack() as ph:
        sbufP, BP = mk_sbuf(ph)
        kf = [BP(f"kf{s}", [128, T_all], BF16) for s in range(2)]
        rf = [BP(f"rf{s}", [128, T_own], BF16) for s in range(2)]
        kt = [BP(f"kt{s}", [128, T_all], BF16) for s in range(2)]
        vt = [BP(f"vt{s}", [128, T_all], BF16) for s in range(2)]
        mats = [BP(f"mats{s}", [128, NMAT * 128], BF16) for s in range(2)]
        matz = [[BP(f"matz{s}_{hh}", [128, (NKK + 1) * 128], BF16)
                 for hh in range(2)] for s in range(2)]
        skct = [BP(f"skct{s}", [128, 128], BF16) for s in range(2)]
        matscr = [BP(f"matscr{s}", [128, 128], F32) for s in range(2)]
        idn_sb = BP("idn_sb", [128, 128], BF16)
        hm_sb = BP("hm_sb", [128, 2], F32)
        tfz = [BP(f"tfz{s}", [128, 2], F32) for s in range(2)]
        ksc = [[BP(f"ksc{s}_{hh}", [128, NKK * T_own], BF16) for hh in range(2)]
               for s in range(2)]
        rsc = [BP(f"rsc{s}", [128, NRK * T_own], BF16) for s in range(2)]
        rsiz = [[BP(f"rsiz{s}_{hh}", [128, T_own], BF16) for hh in range(2)]
                for s in range(2)]
        kut = [BP(f"kut{s}", [128, T_all], BF16) for s in range(2)]
        ktfz = [[BP(f"ktfz{s}_{hh}", [128, T_own], BF16) for hh in range(2)]
                for s in range(2)]
        Ssb = [BP(f"S{s}", [128, 64], F32) for s in range(2)]
        Sb16 = [BP(f"Sb{s}", [128, 64], BF16) for s in range(2)]
        stg = [BP(f"stg{i}", [128, 512], BF16) for i in range(5)]
        xnst = BP("xnst", [128, 256], F32)

        def matv(s, nm):
            i = RAMP_NAMES.index(nm)
            return mats[s].ap[:, 128 * i:128 * (i + 1)]

        def matzv(s, hh, nm):
            i = (KKINDS + ["ri"]).index(nm)
            return matz[s][hh].ap[:, 128 * i:128 * (i + 1)]

        p.cop("v", lambda E: E.tensor_copy(
            idn_sb.ap, masks_sb.ap[:, 128 * MASK_NAMES.index("mI"):
                                   128 * (MASK_NAMES.index("mI") + 1)]),
            reads=(masks_sb,), writes=(idn_sb,))
        p.cop("v", lambda E: E.memset(hm_sb.ap, 0.0), writes=(hm_sb,))
        p.cop("v", lambda E: E.memset(hm_sb.ap[0:64, 0:1], 1.0),
              writes=(hm_sb,))
        p.cop("v", lambda E: E.memset(hm_sb.ap[64:128, 1:2], 1.0),
              writes=(hm_sb,))

        psA, psB_, psC, psD, psE = PS[0], PS[1], PS[2], PS[3], PS[4]
        psO, psU = PS[5], PS[6]

        for grp in range(NG) if "wkv" in PH else ():
            for s in range(2):
                pr = 2 * grp + s
                p.dma("s", lambda E, o=kf[s].ap, q=pr: E.dma_start_transpose(
                    o, k_hbm.ap[:, 128 * q:128 * q + 128]),
                    reads=tuple(k_hbm.rng(0, T_all, 128 * pr, 128 * pr + 128)),
                    writes=(kf[s],))
                p.dma("s", lambda E, o=rf[s].ap, q=pr: E.dma_start_transpose(
                    o, r_hbm.ap[:, 128 * q:128 * q + 128]),
                    reads=tuple(r_hbm.rng(0, T_own, 128 * pr, 128 * pr + 128)),
                    writes=(rf[s],))
                p.dma("y", lambda E, o=kt[s].ap.rearrange(
                    "p (n c) -> p n c", n=NCH), q=pr: E.dma_start(
                    out=o, in_=k_hbm.ap.rearrange("(n p) c -> p n c", p=128)
                    [:, :, 128 * q:128 * q + 128]),
                    reads=tuple(k_hbm.rng(0, T_all, 128 * pr, 128 * pr + 128)),
                    writes=(kt[s],))
                p.dma("y", lambda E, o=vt[s].ap.rearrange(
                    "p (n c) -> p n c", n=NCH), q=pr: E.dma_start(
                    out=o, in_=v_hbm.ap.rearrange("(n p) c -> p n c", p=128)
                    [:, :, 128 * q:128 * q + 128]),
                    reads=tuple(v_hbm.rng(0, T_all, 128 * pr, 128 * pr + 128)),
                    writes=(vt[s],))
                for nm in RAMP_NAMES:
                    p.cop("v", lambda E, o=matscr[s].ap, r=ramp(nm),
                          e=E_sb.ap[:, pr:pr + 1]: E.tensor_scalar(
                          o, r, e, None, ALU.mult),
                          reads=(ramps_sb, E_sb), writes=(matscr[s],))
                    p.cop("s", lambda E, o=matv(s, nm), i=matscr[s].ap:
                          E.activation(o, i, AF.Exp, scale=-1.0),
                          reads=(matscr[s],), writes=(mats[s],))
                for hh in range(2):
                    for nm in KKINDS + ["ri"]:
                        p.cop("v", lambda E, o=matzv(s, hh, nm),
                              i=matv(s, nm), m=hm_sb.ap[:, hh:hh + 1]:
                              E.tensor_scalar(o, i, m, None, ALU.mult),
                              reads=(mats[s], hm_sb), writes=(matz[s][hh],))
                p.cop("v", lambda E, o=tfz[s].ap,
                      i=tfp_sb.ap[:, pr:pr + 1].to_broadcast([128, 2]),
                      m=hm_sb.ap: E.tensor_tensor(o, i, m, ALU.mult),
                      reads=(tfp_sb, hm_sb), writes=(tfz[s],))
                p.cop("t", lambda E, o=PT.ap[:, 0:128], i=matv(s, "ku"):
                      E.transpose(o, i, idn_sb.ap),
                      reads=(mats[s], idn_sb), writes=(PT,))
                p.cop("v", lambda E, o=skct[s].ap, i=PT.ap[:, 0:128]:
                      E.tensor_copy(o, i), reads=(PT,), writes=(skct[s],))
                nto = T_own // 128
                for hh in range(2):
                    for ki in range(NKK):
                        nm = KKINDS[ki]
                        p.cop("v", lambda E,
                              o=ksc[s][hh].ap[:, ki * T_own:(ki + 1) * T_own]
                              .rearrange("p (n c) -> p n c", c=128),
                              i=kf[s].ap[:, HALO:T_all]
                              .rearrange("p (n c) -> p n c", c=128),
                              m=matzv(s, hh, nm).unsqueeze(1)
                              .to_broadcast([128, nto, 128]):
                              E.tensor_tensor(o, i, m, ALU.mult),
                              reads=(kf[s], matz[s][hh]), writes=(ksc[s][hh],))
                    p.cop("v", lambda E,
                          o=rsiz[s][hh].ap.rearrange("p (n c) -> p n c", c=128),
                          i=rf[s].ap.rearrange("p (n c) -> p n c", c=128),
                          m=matzv(s, hh, "ri").unsqueeze(1)
                          .to_broadcast([128, nto, 128]):
                          E.tensor_tensor(o, i, m, ALU.mult),
                          reads=(rf[s], matz[s][hh]), writes=(rsiz[s][hh],))
                    p.cop("v", lambda E, o=ktfz[s][hh].ap,
                          i=kf[s].ap[:, HALO:T_all],
                          t=tfz[s].ap[:, hh:hh + 1]: E.tensor_scalar(
                          o, i, t, None, ALU.mult),
                          reads=(kf[s], tfz[s]), writes=(ktfz[s][hh],))
                for ri_ in range(NRK):
                    nm = RKINDS[ri_]
                    p.cop("v", lambda E,
                          o=rsc[s].ap[:, ri_ * T_own:(ri_ + 1) * T_own]
                          .rearrange("p (n c) -> p n c", c=128),
                          i=rf[s].ap.rearrange("p (n c) -> p n c", c=128),
                          m=matv(s, nm).unsqueeze(1)
                          .to_broadcast([128, nto, 128]):
                          E.tensor_tensor(o, i, m, ALU.mult),
                          reads=(rf[s], mats[s]), writes=(rsc[s],))
                p.cop("v", lambda E,
                      o=kut[s].ap.rearrange("p (n c) -> p n c", c=128),
                      i=kt[s].ap.rearrange("p (n c) -> p n c", c=128),
                      m=skct[s].ap.unsqueeze(1).to_broadcast([128, NCH, 128]):
                      E.tensor_tensor(o, i, m, ALU.mult),
                      reads=(kt[s], skct[s]), writes=(kut[s],))
                p.cop("v", lambda E, o=Ssb[s].ap: E.memset(o, 0.0),
                      writes=(Ssb[s],))
                p.cop("v", lambda E, o=Sb16[s].ap: E.memset(o, 0.0),
                      writes=(Sb16[s],))

            def ksz(s, hh, nm, j0, jn):
                ki = KKINDS.index(nm)
                return ksc[s][hh].ap[:, ki * T_own + j0:ki * T_own + j0 + jn]

            def rsp(s, nm, i0, in_):
                ri_ = RKINDS.index(nm)
                return rsc[s].ap[:, ri_ * T_own + i0:ri_ * T_own + i0 + in_]

            for ch in (range(NCH) if "wkv2" in PH else ()):
                tok0 = ch * C
                owned = ch >= NCH_H and "wkv3" in PH
                oc0 = tok0 - HALO
                if owned:
                    p.cop("v", lambda E: E.memset(psE.ap, 0.0), writes=(psE,))
                    args = {psA.name: [], psB_.name: [], psC.name: [],
                            psD.name: [], psE.name: []}
                    bankof = {b.name: b for b in (psA, psB_, psC, psD, psE)}
                    def amm(bank, stat_ap, mov_ap, rbufs, region, bp=0):
                        args[bank.name].append((stat_ap, mov_ap, rbufs, region,
                                                bp))
                    for s in range(2):
                        for hh in range(2):
                            q = 2 * s + hh
                            qc = slice(128 * q, 128 * q + 128)
                            kb = ksc[s][hh]
                            vb = (kb, rsc[s])
                            amm(psA, ksz(s, hh, "k4c", oc0, 128),
                                rsp(s, "r4c", oc0, 128), vb, psA.ap[:, qc])
                            amm(psB_, ksz(s, hh, "kL4", oc0, 128),
                                rsp(s, "rL4", oc0, 128), vb, psB_.ap[:, qc])
                            amm(psC, ksz(s, hh, "k8", oc0, 128),
                                rsp(s, "r8", oc0, 128), vb, psC.ap[:, qc])
                            amm(psD, ktfz[s][hh].ap[:, oc0:oc0 + 128],
                                rf[s].ap[:, oc0:oc0 + 128],
                                (ktfz[s][hh], rf[s]), psD.ap[:, qc])
                            amm(psE, ksz(s, hh, "k64", oc0, 64),
                                rsp(s, "r64", oc0 + 64, 64), vb,
                                psE.ap[0:64, 128 * q + 64:128 * q + 128])
                            for wloc in range(2):
                                j0 = 64 * wloc
                                amm(psE, ksz(s, hh, "k32", oc0 + j0, 32),
                                    rsp(s, "r32", oc0 + j0 + 32, 32), vb,
                                    psE.ap[j0:j0 + 32,
                                           128 * q + j0 + 32:128 * q + j0 + 64],
                                    j0)
                            for wloc in range(4):
                                j0 = 32 * wloc
                                amm(psE, ksz(s, hh, "k16", oc0 + j0, 16),
                                    rsp(s, "r16", oc0 + j0 + 16, 16), vb,
                                    psE.ap[j0:j0 + 16,
                                           128 * q + j0 + 16:128 * q + j0 + 32],
                                    j0)
                    for bname, mms in args.items():
                        bank = bankof[bname]
                        ise = bname == psE.name
                        if ise and "wkvAE" not in PH:
                            continue
                        if not ise and "wkvA4" not in PH:
                            continue
                        for mi, (stat_ap, mov_ap, rbufs, region, bp) in \
                                enumerate(mms):
                            st = (mi == 0) and not ise
                            sp = (mi == len(mms) - 1) and not ise
                            tp = (0, bp) if bp else None
                            p.cop("t", lambda E, o=region, a=stat_ap, b=mov_ap,
                                  st_=st, sp_=sp, sk=ise, tp_=tp: E.matmul(
                                  o, a, b, start=st_, stop=sp_,
                                  skip_group_check=sk, tile_position=tp_),
                                  reads=rbufs, writes=(bank,),
                                  waw_free=(mi != 0))
                    evlist = []
                    if "wkvA4" in PH and "wkvEv" in PH:
                        evlist += [(0, psA, "m4"), (1, psB_, "m8"),
                                   (2, psC, "m16"), (3, psD, "mI")]
                    if "wkvAE" in PH and "wkvEv" in PH:
                        evlist += [(4, psE, "mB")]
                    for stgi, bank, mnm in evlist:
                        p.cop("v", lambda E,
                              o=stg[stgi].ap.rearrange("p (n c) -> p n c", c=128),
                              i=bank.ap.rearrange("p (n c) -> p n c", c=128),
                              m=maskv(mnm, 4): E.tensor_tensor(
                              o, i, m, ALU.mult),
                              reads=(bank, masks_sb), writes=(stg[stgi],))
                    omms = [] if "wkv4" in PH else None
                    for s in (range(2) if omms is not None else ()):
                        for hh in range(2):
                            q = 2 * s + hh
                            oreg = psO.ap[:, 64 * q:64 * q + 64]
                            vmov = vt[s].ap[:, tok0 + 64 * hh:tok0 + 64 * hh + 64]
                            for stgi in range(5):
                                omms.append((stg[stgi].ap[:, 128 * q:128 * q + 128],
                                             vmov, (stg[stgi], vt[s]), oreg))
                            omms.append((rsiz[s][hh].ap[:, oc0:oc0 + 128],
                                         Sb16[s].ap,
                                         (rsiz[s][hh], Sb16[s]), oreg))
                    for mi, (stat_ap, mov_ap, rbufs, region) in \
                            enumerate(omms or ()):
                        p.cop("t", lambda E, o=region, a=stat_ap, b=mov_ap,
                              st_=(mi == 0), sp_=(mi == len(omms) - 1):
                              E.matmul(o, a, b, start=st_, stop=sp_),
                              reads=rbufs, writes=(psO,), waw_free=(mi != 0))
                p.cop("v", lambda E: E.memset(psU.ap[:, 0:128], 0.0),
                      writes=(psU,))
                for s in range(2):
                    for hh in range(2):
                        h0 = 64 * hh
                        p.cop("t", lambda E,
                              o=psU.ap[h0:h0 + 64, 64 * s:64 * s + 64],
                              a=kut[s].ap[:, tok0 + h0:tok0 + h0 + 64],
                              b=vt[s].ap[:, tok0 + h0:tok0 + h0 + 64],
                              tp=(0, h0): E.matmul(
                              o, a, b, start=False, stop=False,
                              tile_position=tp, skip_group_check=True),
                              reads=(kut[s], vt[s]), writes=(psU,),
                              waw_free=not (s == 0 and hh == 0))
                for s in range(2):
                    pr = 2 * grp + s
                    p.cop("v", lambda E, o=Ssb[s].ap, i=Ssb[s].ap,
                          c=tdC_sb.ap[:, pr:pr + 1],
                          u=psU.ap[:, 64 * s:64 * s + 64]:
                          E.scalar_tensor_tensor(o, i, c, u, ALU.mult, ALU.add),
                          reads=(Ssb[s], tdC_sb, psU), writes=(Ssb[s],))
                    p.cop("v", lambda E, o=Sb16[s].ap, i=Ssb[s].ap:
                          E.tensor_copy(o, i), reads=(Ssb[s],), writes=(Sb16[s],))
                if owned and "wkv4" in PH:
                    p.cop("s", lambda E, o=xnst.ap, i=psO.ap[:, 0:256]:
                          E.activation(o, i, AF.Copy), reads=(psO,),
                          writes=(xnst,))
                    for s in range(2):
                        pr = 2 * grp + s
                        p.dma("y", lambda E, o=xn_hbm.ap, r=oc0, q=pr,
                              i=xnst.ap[:, 128 * s:128 * s + 128]: E.dma_start(
                            out=o[r:r + 128, 128 * q:128 * q + 128], in_=i),
                            reads=(xnst,),
                            writes=xn_hbm.rng(oc0, oc0 + 128, 128 * pr,
                                              128 * pr + 128))
        p.barrier()

    inv64 = 1.0 / 64.0
    with _ctx.ExitStack() as ph:
        sbufP, BP = mk_sbuf(ph)
        xng = [BP(f"xng{i}", [128, Dm], F32) for i in range(2)]
        gg = [BP(f"gg{i}", [128, Dm], BF16) for i in range(2)]
        sqg = BP("sqg", [128, Dm], F32)
        stg2 = [BP(f"stg2{i}", [128, 8 * H], F32) for i in range(2)]
        yst = [BP(f"yst{i}", [128, Dm], BF16) for i in range(2)]
        gnw_bc = BP("gnw_bc", [128, Dm], F32)
        gnb_bc = BP("gnb_bc", [128, Dm], F32)
        p.dma("y", lambda E: E.dma_start(
            out=gnw_bc.ap, in_=gnw_in.to_broadcast([128, Dm])),
            writes=(gnw_bc,))
        p.dma("y", lambda E: E.dma_start(
            out=gnb_bc.ap, in_=gnb_in.to_broadcast([128, Dm])),
            writes=(gnb_bc,))
        for tt in range(T_own // 128) if "gn" in PH else ():
            par = tt % 2
            xa, st, yb, gb = xng[par], stg2[par], yst[par], gg[par]
            t0 = tt * 128
            p.dma("y", lambda E, o=xa.ap, t=t0: E.dma_start(
                out=o, in_=xn_hbm.ap[t:t + 128, :]),
                reads=tuple(xn_hbm.rng(t0, t0 + 128)), writes=(xa,))
            p.dma("y", lambda E, o=gb.ap, t=t0: E.dma_start(
                out=o, in_=g_hbm.ap[t:t + 128, :]),
                reads=tuple(g_hbm.rng(t0, t0 + 128)), writes=(gb,))
            sums = st.ap[:, 0:H]
            sumsq = st.ap[:, H:2 * H]
            mean = st.ap[:, 2 * H:3 * H]
            varv = st.ap[:, 3 * H:4 * H]
            sd = st.ap[:, 4 * H:5 * H]
            rstd = st.ap[:, 5 * H:6 * H]
            m2 = st.ap[:, 6 * H:7 * H]
            xv = xa.ap.rearrange("p (h e) -> p h e", h=H)
            p.cop("v", lambda E, o=sums, i=xv: E.reduce_sum(o, i, AX),
                  reads=(xa,), writes=(st,))
            p.cop("s", lambda E, o=sqg.ap, i=xa.ap: E.activation(
                o, i, AF.Square), reads=(xa,), writes=(sqg,))
            p.cop("v", lambda E, o=sumsq,
                  i=sqg.ap.rearrange("p (h e) -> p h e", h=H):
                  E.reduce_sum(o, i, AX), reads=(sqg,), writes=(st,))
            p.cop("v", lambda E, o=mean, i=sums: E.tensor_single_scalar(
                o, i, inv64, ALU.mult), reads=(st,), writes=(st,))
            p.cop("v", lambda E, o=m2, i=mean: E.tensor_tensor(
                o, i, i, ALU.mult), reads=(st,), writes=(st,))
            p.cop("v", lambda E, o=varv, i=sumsq: E.tensor_single_scalar(
                o, i, inv64, ALU.mult), reads=(st,), writes=(st,))
            p.cop("v", lambda E, o=varv, a=varv, b=m2: E.tensor_sub(o, a, b),
                  reads=(st,), writes=(st,))
            p.cop("s", lambda E, o=sd, i=varv: E.activation(
                o, i, AF.Sqrt, bias=eps_sb.ap), reads=(st, eps_sb), writes=(st,))
            p.cop("v", lambda E, o=rstd, i=sd: E.reciprocal(o, i),
                  reads=(st,), writes=(st,))
            meanb = mean.unsqueeze(2).to_broadcast([128, H, 64])
            rstdb = rstd.unsqueeze(2).to_broadcast([128, H, 64])
            p.cop("v", lambda E, o=xv, i=xv, m=meanb: E.tensor_sub(o, i, m),
                  reads=(xa, st), writes=(xa,))
            p.cop("v", lambda E, o=xv, i=xv, m=rstdb: E.tensor_tensor(
                o, i, m, ALU.mult), reads=(xa, st), writes=(xa,))
            p.cop("v", lambda E, o=xa.ap, i=xa.ap, m=gnw_bc.ap: E.tensor_tensor(
                o, i, m, ALU.mult), reads=(xa, gnw_bc), writes=(xa,))
            p.cop("v", lambda E, o=xa.ap, i=xa.ap, m=gnb_bc.ap: E.tensor_add(
                o, i, m), reads=(xa, gnb_bc), writes=(xa,))
            p.cop("v", lambda E, o=yb.ap, i=xa.ap, m=gb.ap: E.tensor_tensor(
                o, i, m, ALU.mult), reads=(xa, gb), writes=(yb,))
            p.dma("y", lambda E, o=y_hbm.ap, t=t0, i=yb.ap: E.dma_start(
                out=o[t:t + 128, :], in_=i), reads=(yb,),
                writes=y_hbm.rng(t0, t0 + 128))
        p.barrier()

    with _ctx.ExitStack() as ph:
        sbufP, BP = mk_sbuf(ph)
        yT_t = sbufP("yT_sb", [128, ND * T_own], BF16)
        yT_v = yT_t[:].rearrange("p (n c) -> p n c", n=ND)
        yT_sb = [Buf(yT_v[:, d, :], f"yT{d}") for d in range(ND)]
        woslab_t = [sbufP(f"wosl{i}", [128, ND * OBW], BF16) for i in range(2)]
        woslab = [Buf(woslab_t[i][:].rearrange("p (n c) -> p n c", n=ND),
                      f"wosl{i}") for i in range(2)]
        xp = [BP(f"xp{i}", [128, OBW], F32) for i in range(2)]
        x2st = [BP(f"x2st{i}", [128, OBW], F32) for i in range(2)]
        for d in range(ND) if "op" in PH else ():
            p.dma("s", lambda E, o=yT_sb[d].ap, dd=d: E.dma_start_transpose(
                o, y_hbm.ap[:, 128 * dd:128 * dd + 128]),
                reads=tuple(y_hbm.rng(0, T_own)), writes=(yT_sb[d],))
        for ob in range(NO) if "op" in PH else ():
            slab = woslab[ob % 2]
            p.dma("y", lambda E, o=slab.ap[:, :, 0:OBW], b=ob: E.dma_start(
                out=o, in_=wbf["wo"].ap.rearrange("(n p) c -> p n c", p=128)
                [:, :, OBW * b:OBW * b + OBW]),
                reads=tuple(wbf["wo"].rng(0, Dm)), writes=(slab,))
            for ts in range(T_own // 128):
                bank = PS[ts % 2]
                for kk in range(ND):
                    p.cop("t", lambda E, o=bank.ap[:, 0:OBW],
                          a=yT_sb[kk].ap[:, 128 * ts:128 * ts + 128],
                          b=slab.ap[:, kk, 0:OBW], st=(kk == 0),
                          sp=(kk == ND - 1):
                          E.matmul(o, a, b, start=st, stop=sp),
                          reads=(yT_sb[kk], slab), writes=(bank,),
                          waw_free=(kk != 0))
                xpb = xp[ts % 2]
                x2b = x2st[ts % 2]
                p.dma("y", lambda E, o=xpb.ap, t=ts * 128, b=ob: E.dma_start(
                    out=o, in_=x_in[HALO + t:HALO + t + 128,
                                    OBW * b:OBW * b + OBW]),
                    writes=(xpb,))
                p.cop("v", lambda E, o=x2b.ap, i=bank.ap[:, 0:OBW], m=xpb.ap:
                      E.tensor_add(o, i, m), reads=(bank, xpb), writes=(x2b,))
                p.dma("y", lambda E, o=x2_hbm.ap, t=ts * 128, b=ob, i=x2b.ap:
                      E.dma_start(out=o[t:t + 128, OBW * b:OBW * b + OBW],
                                  in_=i), reads=(x2b,),
                      writes=x2_hbm.rng(ts * 128, ts * 128 + 128, ob * OBW,
                                        ob * OBW + OBW))
        p.barrier()

    with _ctx.ExitStack() as ph:
        sbufP, BP = mk_sbuf(ph)
        xa_sb = [BP(f"xb{i}", [128, Dm], F32) for i in range(2)]
        sq_sb = [BP(f"sqb{i}", [128, Dm], F32) for i in range(2)]
        stat_sb = [BP(f"stb{i}", [128, 8], F32) for i in range(2)]
        ln_st = [BP(f"lnb{i}", [128, Dm], BF16) for i in range(2)]
        for tt in range(T_own // 128) if "ln2" in PH else ():
            par = tt % 2
            emit_ln(x2_hbm.ap, tt * 128, ln2_hbm, tt * 128,
                    (xa_sb[par], stat_sb[par], ln_st[par], sq_sb[par]),
                    src_d=x2_hbm)
        p.barrier()

    NHB = (F + 511) // 512
    with _ctx.ExitStack() as ph:
        sbufP, BP = mk_sbuf(ph)
        ln2T_t = sbufP("ln2T_sb", [128, ND * TB2], BF16)
        ln2T_v = ln2T_t[:].rearrange("p (n c) -> p n c", n=ND)
        ln2T_sb = [Buf(ln2T_v[:, d, :], f"l2T{d}") for d in range(ND)]
        w1sl_t = [sbufP(f"w1sl{i}", [128, ND * 512], BF16) for i in range(2)]
        w1sl = [Buf(w1sl_t[i][:].rearrange("p (n c) -> p n c", n=ND), f"w1sl{i}")
                for i in range(2)]
        w3sl_t = [sbufP(f"w3sl{i}", [128, ND * 512], BF16) for i in range(2)]
        w3sl = [Buf(w3sl_t[i][:].rearrange("p (n c) -> p n c", n=ND), f"w3sl{i}")
                for i in range(2)]
        sast = [BP(f"sast{i}", [128, 512], F32) for i in range(2)]
        hst = [BP(f"hst{i}", [128, 512], BF16) for i in range(2)]
        for fb in range(NFB) if "ffa" in PH else ():
            ft0 = fb * TB2
            for d in range(ND):
                p.dma("s", lambda E, o=ln2T_sb[d].ap, dd=d, t=ft0:
                      E.dma_start_transpose(
                    o, ln2_hbm.ap[t:t + TB2, 128 * dd:128 * dd + 128]),
                    reads=tuple(ln2_hbm.rng(ft0, ft0 + TB2)),
                    writes=(ln2T_sb[d],))
            for hb in range(NHB):
                hw = min(512, F - 512 * hb)
                sl1, sl3 = w1sl[hb % 2], w3sl[hb % 2]
                p.dma("y", lambda E, o=sl1.ap[:, :, 0:hw], b=hb, w_=hw:
                      E.dma_start(
                    out=o, in_=w1bf.ap.rearrange("(n p) c -> p n c", p=128)
                    [:, :, 512 * b:512 * b + w_]),
                    reads=tuple(w1bf.rng(0, Dm, 512 * hb, 512 * hb + hw)),
                    writes=(sl1,))
                p.dma("y", lambda E, o=sl3.ap[:, :, 0:hw], b=hb, w_=hw:
                      E.dma_start(
                    out=o, in_=w3bf.ap.rearrange("(n p) c -> p n c", p=128)
                    [:, :, 512 * b:512 * b + w_]),
                    reads=tuple(w3bf.rng(0, Dm, 512 * hb, 512 * hb + hw)),
                    writes=(sl3,))
                for ts in range(TB2 // 128):
                    bka = PS[2 * (ts % 2)]
                    bkb = PS[2 * (ts % 2) + 1]
                    for kk in range(ND):
                        p.cop("t", lambda E, o=bka.ap[:, 0:hw],
                              a=ln2T_sb[kk].ap[:, 128 * ts:128 * ts + 128],
                              b=sl1.ap[:, kk, 0:hw], st=(kk == 0),
                              sp=(kk == ND - 1):
                              E.matmul(o, a, b, start=st, stop=sp),
                              reads=(ln2T_sb[kk], sl1), writes=(bka,),
                              waw_free=(kk != 0))
                    for kk in range(ND):
                        p.cop("t", lambda E, o=bkb.ap[:, 0:hw],
                              a=ln2T_sb[kk].ap[:, 128 * ts:128 * ts + 128],
                              b=sl3.ap[:, kk, 0:hw], st=(kk == 0),
                              sp=(kk == ND - 1):
                              E.matmul(o, a, b, start=st, stop=sp),
                              reads=(ln2T_sb[kk], sl3), writes=(bkb,),
                              waw_free=(kk != 0))
                    sab = sast[ts % 2]
                    hb_st = hst[ts % 2]
                    p.cop("s", lambda E, o=sab.ap[:, 0:hw], i=bka.ap[:, 0:hw]:
                          E.activation(o, i, AF.Silu), reads=(bka,),
                          writes=(sab,))
                    p.cop("v", lambda E, o=hb_st.ap[:, 0:hw],
                          i=sab.ap[:, 0:hw], m=bkb.ap[:, 0:hw]:
                          E.tensor_tensor(o, i, m, ALU.mult),
                          reads=(sab, bkb), writes=(hb_st,))
                    p.dma("y", lambda E, o=h_hbm.ap, t=ft0 + ts * 128, b=hb,
                          w_=hw, i=hb_st.ap: E.dma_start(
                        out=o[t:t + 128, 512 * b:512 * b + w_],
                        in_=i[:, 0:w_]), reads=(hb_st,),
                        writes=h_hbm.rng(ft0 + ts * 128, ft0 + ts * 128 + 128,
                                         512 * hb, 512 * hb + hw))
        p.barrier()

    with _ctx.ExitStack() as ph:
        sbufP, BP = mk_sbuf(ph)
        hT_t = sbufP("hT_sb", [128, NQ2 * TB2], BF16)
        hT_v = hT_t[:].rearrange("p (n c) -> p n c", n=NQ2)
        hT_sb = [Buf(hT_v[:, d, :], f"hT{d}") for d in range(NQ2)]
        w2sl_t = [sbufP(f"w2sl{i}", [128, NQ2 * OB2], BF16) for i in range(2)]
        w2sl = [Buf(w2sl_t[i][:].rearrange("p (n c) -> p n c", n=NQ2),
                    f"w2sl{i}") for i in range(2)]
        xp2 = [BP(f"xp2{i}", [128, OB2], F32) for i in range(2)]
        outst = [BP(f"outst{i}", [128, OB2], F32) for i in range(2)]
        for fb in range(NFB) if "ffw2" in PH else ():
            ft0 = fb * TB2
            for d in range(NQ2):
                p.dma("s", lambda E, o=hT_sb[d].ap, dd=d, t=ft0:
                      E.dma_start_transpose(
                    o, h_hbm.ap[t:t + TB2, 128 * dd:128 * dd + 128]),
                    reads=tuple(h_hbm.rng(ft0, ft0 + TB2, 128 * d,
                                          128 * d + 128)),
                    writes=(hT_sb[d],))
            for ob in range(NO2):
                slab = w2sl[ob % 2]
                p.dma("y", lambda E, o=slab.ap[:, :, 0:OB2], b=ob: E.dma_start(
                    out=o, in_=w2bf.ap.rearrange("(n p) c -> p n c", p=128)
                    [:, :, OB2 * b:OB2 * b + OB2]),
                    reads=tuple(w2bf.rng(0, F)), writes=(slab,))
                for ts in range(TB2 // 128):
                    bank = PS[ts % 2]
                    for kk in range(NQ2):
                        p.cop("t", lambda E, o=bank.ap[:, 0:OB2],
                              a=hT_sb[kk].ap[:, 128 * ts:128 * ts + 128],
                              b=slab.ap[:, kk, 0:OB2], st=(kk == 0),
                              sp=(kk == NQ2 - 1):
                              E.matmul(o, a, b, start=st, stop=sp),
                              reads=(hT_sb[kk], slab), writes=(bank,),
                              waw_free=(kk != 0))
                    xpb = xp2[ts % 2]
                    osb = outst[ts % 2]
                    t0 = ft0 + ts * 128
                    p.dma("y", lambda E, o=xpb.ap, t=t0, b=ob: E.dma_start(
                        out=o, in_=x2_hbm.ap[t:t + 128, OB2 * b:OB2 * b + OB2]),
                        reads=tuple(x2_hbm.rng(t0, t0 + 128, ob * OB2,
                                               ob * OB2 + OB2)),
                        writes=(xpb,))
                    p.cop("v", lambda E, o=osb.ap, i=bank.ap[:, 0:OB2],
                          m=xpb.ap: E.tensor_add(o, i, m),
                          reads=(bank, xpb), writes=(osb,))
                    p.dma("y", lambda E, o=out_c, t=t0, b=ob, i=osb.ap:
                          E.dma_start(out=o[t:t + 128, OB2 * b:OB2 * b + OB2],
                                      in_=i), reads=(osb,))
        p.barrier()

    p.finish("y")

    stack = emit_all(nc, p)
    return nc



def host_prepare(cfg, inputs):
    n_cores = cfg["n_cores"]
    Dm, F, T_own, HALO = cfg["Dm"], cfg["F"], cfg["T_own"], cfg["HALO"]
    T_all = T_own + HALO
    ND = Dm // 128
    P2 = Dm // 128
    NQ1 = Dm // 128
    NQ2 = F // 128

    x = np.asarray(inputs["L0_attn_hidden"], np.float32).reshape(-1, Dm)
    base = {}
    for nm, w in (("wkT", "L0_attn_w_key"), ("wvT", "L0_attn_w_value"),
                  ("wrT", "L0_attn_w_receptance"), ("wgT", "L0_attn_w_gate"),
                  ("woT", "L0_attn_w_output")):
        base[nm] = np.ascontiguousarray(np.asarray(inputs[w], np.float32).T)
    for nm, m in (("mk", "L0_attn_time_mix_key"), ("mv", "L0_attn_time_mix_value"),
                  ("mr", "L0_attn_time_mix_receptance"),
                  ("mg", "L0_attn_time_mix_gate")):
        base[nm] = np.ascontiguousarray(
            np.asarray(inputs[m], np.float32).reshape(ND, 128).T)
    td = np.asarray(inputs["L0_attn_time_decay"], np.float32)
    tf = np.asarray(inputs["L0_attn_time_faaaa"], np.float32)
    base["td_pair"] = np.ascontiguousarray(td.reshape(P2, 128).T)
    base["tf_pair"] = np.ascontiguousarray(tf.reshape(P2, 128).T)
    base["gnw"] = np.asarray(inputs["L0_attn_gn_weight"], np.float32).reshape(1, Dm)
    base["gnb"] = np.asarray(inputs["L0_attn_gn_bias"], np.float32).reshape(1, Dm)
    base["ramps"] = make_ramps()
    base["masks"] = make_masks()
    for nm, wid, nq in (("1", F, NQ1), ("3", F, NQ1), ("2", Dm, NQ2)):
        w = np.asarray(inputs["L0_ffn_w" + nm], np.int32)
        s = np.asarray(inputs["L0_ffn_s" + nm], np.float32)
        base["w" + nm + "p"] = np.ascontiguousarray(
            w.reshape(wid, nq, 64).transpose(2, 1, 0))
        base["s" + nm + "d"] = np.ascontiguousarray(
            s.reshape(wid, nq, 2).transpose(1, 2, 0))

    in_maps = []
    for c in range(n_cores):
        m = dict(base)
        xc = np.zeros((T_all, Dm), np.float32)
        start = c * T_own - HALO
        lo = max(0, -start)
        xc[lo:] = x[start + lo: c * T_own + T_own]
        m["x_c"] = xc
        in_maps.append(m)
    return in_maps


def kernel(**inputs):
    cfg = full_cfg()
    nc = build_program(cfg)
    in_maps = host_prepare(cfg, inputs)
    res = run_bass_kernel_spmd(nc, in_maps, core_ids=list(range(cfg["n_cores"])))
    outs = [np.asarray(r["out_c"], np.float32) for r in res.results]
    full = np.concatenate(outs, axis=0)
    return full.reshape(1, cfg["n_cores"] * cfg["T_own"], cfg["Dm"])


# version 05
# speedup vs baseline: 1.5933x; 1.5933x over previous
import numpy as np
import ml_dtypes
import concourse.bass as bass
import concourse.mybir as mybir
from concourse.bass_utils import run_bass_kernel_spmd

F32 = mybir.dt.float32
BF16 = mybir.dt.bfloat16
I32 = mybir.dt.int32
AF = mybir.ActivationFunctionType
ALU = mybir.AluOpType
AX = mybir.AxisListType.X
BF16NP = ml_dtypes.bfloat16


def full_cfg():
    return dict(n_cores=8, Dm=4096, F=11008, T_own=1024, HALO=512, TB=512)



class SemT:
    __slots__ = ("name", "cnt", "h")

    def __init__(self, name):
        self.name, self.cnt, self.h = name, 0, None

    def take(self, n=1):
        self.cnt += n
        return self.cnt


class Rec:
    __slots__ = ("fn", "incs", "tagval")

    def __init__(self, fn):
        self.fn, self.incs, self.tagval = fn, [], None


class Buf:
    __slots__ = ("ap", "w", "readers", "name")

    def __init__(self, ap, name=""):
        self.ap, self.w, self.readers, self.name = ap, None, {}, name


class DBuf:

    def __init__(self, ap, shape, rgran, cgran, name=""):
        self.ap = ap
        self.shape = shape
        self.rg, self.cg = rgran, cgran
        self.name = name
        self.gr = {}

    def _g(self, ri, ci):
        key = (ri, ci)
        b = self.gr.get(key)
        if b is None:
            b = Buf(None, f"{self.name}[{ri},{ci}]")
            self.gr[key] = b
        return b

    def rng(self, r0, r1, c0=0, c1=None):
        if c1 is None:
            c1 = self.shape[1]
        out = []
        for ri in range(r0 // self.rg, (r1 - 1) // self.rg + 1):
            for ci in range(c0 // self.cg, (c1 - 1) // self.cg + 1):
                out.append(self._g(ri, ci))
        return out


QS = ("t", "v", "s", "g", "y")


class Gen:
    __slots__ = ("sem", "end")

    def __init__(self, sem):
        self.sem, self.end = sem, None


class Plan:
    DMA_POOL = 6

    def __init__(self):
        self.q = {k: [] for k in QS}
        self.sems = []
        self.pending = {k: [] for k in QS}
        self.chain = {k: self.new_sem("chain_" + k) for k in ("t", "v", "s", "g")}
        self.waited = {k: {} for k in QS}
        self.dpool = {k: [self.new_sem(f"dma_{k}{i}") for i in range(self.DMA_POOL)]
                      for k in ("y", "s", "g", "v")}
        self.didx = {k: 0 for k in self.dpool}
        self.dopen = {k: None for k in self.dpool}

    def new_sem(self, name):
        s = SemT(name)
        self.sems.append(s)
        return s

    def raw(self, qn, fn):
        rec = Rec(fn)
        self.q[qn].append(rec)
        return rec

    def wait(self, qn, sem, val):
        if val <= 0:
            return
        w = self.waited[qn]
        if w.get(sem.name, 0) >= val:
            return
        w[sem.name] = val
        self.raw(qn, lambda E, s=sem, v=val: E.wait_ge(s.h, v))

    def _resolve(self, dep):
        kind = dep[0]
        if kind == "s":
            return dep[1], dep[2]
        if kind == "d":
            qn, gen = dep[1], dep[2]
            if gen.end is None:
                gen.end = gen.sem.cnt
                if self.dopen[qn] is gen:
                    self.dopen[qn] = None
            return gen.sem, gen.end
        qn, rec = dep[1], dep[2]
        if rec.tagval is None:
            pend = self.pending[qn]
            idx = pend.index(rec)
            ch = self.chain[qn]
            ch.take(idx + 1)
            rec.incs.append((ch, idx + 1))
            for r in pend[: idx + 1]:
                r.tagval = ch.cnt
            del pend[: idx + 1]
        return self.chain[qn], rec.tagval

    def dep_wait(self, qn, dep):
        sem, val = self._resolve(dep)
        self.wait(qn, sem, val)

    def _pre(self, qn, reads, writes, waw_free=False):
        for b in reads:
            if b.w is not None:
                self.dep_wait(qn, b.w)
        for b in writes:
            if not waw_free and b.w is not None:
                self.dep_wait(qn, b.w)
            if not waw_free:
                for d in list(b.readers.values()):
                    self.dep_wait(qn, d)

    def cop(self, qn, fn, reads=(), writes=(), waw_free=False):
        self._pre(qn, reads, writes, waw_free)
        rec = self.raw(qn, fn)
        self.pending[qn].append(rec)
        dep = ("c", qn, rec)
        for b in reads:
            b.readers[qn] = dep
        for b in writes:
            b.w = dep
            b.readers = {}
        return rec

    def dma(self, qn, fn, reads=(), writes=()):
        self._pre(qn, reads, writes)
        gen = self.dopen[qn]
        if gen is None:
            pool = self.dpool[qn]
            sem = pool[self.didx[qn] % len(pool)]
            self.didx[qn] += 1
            if sem.cnt > 0:
                self.wait(qn, sem, sem.cnt)
            gen = Gen(sem)
            self.dopen[qn] = gen
        gen.sem.take(16)
        rec = self.raw(qn, fn)
        rec.incs.append((gen.sem, 16))
        dep = ("d", qn, gen)
        for b in reads:
            b.readers[qn + "dma"] = dep
        for b in writes:
            b.w = dep
            b.readers = {}
        return dep

    def flush_pending(self):
        for qn in ("t", "v", "s", "g"):
            pend = self.pending[qn]
            if pend:
                self._resolve(("c", qn, pend[-1]))

    def barrier(self):
        self.flush_pending()
        for pq in self.dpool:
            g = self.dopen[pq]
            if g is not None:
                g.end = g.sem.cnt
                self.dopen[pq] = None
        targets = [(ch, ch.cnt) for ch in self.chain.values() if ch.cnt]
        for pool in self.dpool.values():
            targets.extend((s, s.cnt) for s in pool if s.cnt)
        for qn in QS:
            for sem, val in targets:
                self.wait(qn, sem, val)

    def finish(self, qn="y"):
        for pq, pool in self.dpool.items():
            g = self.dopen[pq]
            if g is not None:
                g.end = g.sem.cnt
                self.dopen[pq] = None
            for s in pool:
                self.wait(qn, s, s.cnt)
        for k, ch in self.chain.items():
            self.wait(qn, ch, ch.cnt)


def emit_all(nc, plan):
    import contextlib
    stack = contextlib.ExitStack()
    for s in plan.sems:
        s.h = stack.enter_context(nc.semaphore(s.name))
    block = stack.enter_context(nc.Block())

    def runq(E, qn):
        for rec in plan.q[qn]:
            ins = rec.fn(E)
            for sem, n in rec.incs:
                ins.then_inc(sem.h, n)

    @block.sync
    def _(E):
        runq(E, "y")

    @block.vector
    def _(E):
        runq(E, "v")

    @block.scalar
    def _(E):
        runq(E, "s")

    @block.gpsimd
    def _(E):
        runq(E, "g")

    @block.tensor
    def _(E):
        runq(E, "t")

    return stack



RAMP_NAMES = ["r4c", "k4c", "rL4", "kL4", "r8", "k8", "r16", "k16",
              "r32", "k32", "r64", "k64", "ri", "ku"]
BIGR = 1.0e4
CHUNK = 128


def make_ramps():
    i = np.arange(CHUNK).astype(np.float64)
    Z = lambda cond, g: np.where(cond, g, BIGR)
    g = {
        "r4c": (i % 4) - 2.0,
        "k4c": 1.0 - (i % 4),
        "rL4": Z(i % 8 >= 4, (i % 8) - 4.0),
        "kL4": Z(i % 8 < 4, 3.0 - (i % 8)),
        "r8":  Z(i % 16 >= 8, (i % 16) - 8.0),
        "k8":  Z(i % 16 < 8, 7.0 - (i % 16)),
        "r16": Z(i % 32 >= 16, (i % 32) - 16.0),
        "k16": Z(i % 32 < 16, 15.0 - (i % 32)),
        "r32": Z(i % 64 >= 32, (i % 64) - 32.0),
        "k32": Z(i % 64 < 32, 31.0 - (i % 64)),
        "r64": Z(i % 128 >= 64, (i % 128) - 64.0),
        "k64": Z(i % 128 < 64, 63.0 - (i % 128)),
        "ri":  i,
        "ku":  (CHUNK - 1.0) - i,
    }
    return np.stack([g[n] for n in RAMP_NAMES]).astype(np.float32)


MASK_NAMES = ["m4", "m8", "m16", "mI", "mB"]


def make_masks():
    jj = np.arange(CHUNK)[:, None]
    ii = np.arange(CHUNK)[None, :]
    m4 = ((jj // 4 == ii // 4) & (jj < ii)).astype(np.float32)
    m8 = (jj // 8 == ii // 8).astype(np.float32)
    m16 = (jj // 16 == ii // 16).astype(np.float32)
    mI = (jj == ii).astype(np.float32)
    mB = np.zeros((CHUNK, CHUNK))
    for L in (16, 32, 64):
        W = 2 * L
        mB += ((jj // W == ii // W) & (jj % W < L) & (ii % W >= L))
    mB = (mB > 0).astype(np.float32)
    return np.stack([m4, m8, m16, mI, mB]).astype(BF16NP)



def build_program(cfg):
    Dm, F, T_own, HALO, TB = (cfg["Dm"], cfg["F"], cfg["T_own"], cfg["HALO"],
                              cfg["TB"])
    T_all = T_own + HALO
    H = Dm // 64
    P2 = H // 2
    NG = (P2 + 1) // 2
    ND = Dm // 128
    OBW = min(512, Dm)
    NO = Dm // OBW
    C = CHUNK
    NCH = T_all // C
    NCH_H = HALO // C
    NB = T_all // TB
    NT = TB // 128
    NQ1 = Dm // 128
    NQ2 = F // 128
    OB2 = min(256, Dm)
    NO2 = Dm // OB2
    TB2 = min(512, T_own)
    NFB = T_own // TB2
    assert HALO % TB == 0 and T_all % TB == 0 and TB % 128 == 0
    assert T_own % C == 0 and HALO % C == 0 and P2 % 2 == 0

    nc = bass.Bass("TRN2", target_bir_lowering=False, debug=False)

    def ein(name, shape, dt):
        return nc.dram_tensor(name, list(shape), dt, kind="ExternalInput").ap()

    x_in = ein("x_c", [T_all, Dm], F32)
    wT = {w: ein(w + "T", [Dm, Dm], F32) for w in ("wk", "wv", "wr", "wg", "wo")}
    mix_in = {m: ein(m, [128, ND], F32) for m in ("mk", "mv", "mr", "mg")}
    td_in = ein("td_pair", [128, P2], F32)
    tf_in = ein("tf_pair", [128, P2], F32)
    gnw_in = ein("gnw", [1, Dm], F32)
    gnb_in = ein("gnb", [1, Dm], F32)
    ramps_in = ein("ramps", [len(RAMP_NAMES), 128], F32)
    masks_in = ein("masks", [len(MASK_NAMES), 128, 128], BF16)
    w1p = ein("w1p", [64, NQ1, F], I32)
    w3p = ein("w3p", [64, NQ1, F], I32)
    w2p = ein("w2p", [64, NQ2, Dm], I32)
    s1d = ein("s1d", [NQ1, 2, F], F32)
    s3d = ein("s3d", [NQ1, 2, F], F32)
    s2d = ein("s2d", [NQ2, 2, Dm], F32)
    out_c = nc.dram_tensor("out_c", [T_own, Dm], F32, kind="ExternalOutput").ap()

    def dscr(name, shape, dt):
        return nc.dram_tensor(name, list(shape), dt, kind="Internal").ap()

    def dbuf(name, shape, dt, rgran, cgran):
        return DBuf(dscr(name, shape, dt), shape, rgran, cgran, name)

    ln1_hbm = dbuf("ln1_hbm", [T_all + 128, Dm], BF16, 128, Dm)
    k_hbm = dbuf("k_hbm", [T_all, Dm], BF16, 128, OBW)
    v_hbm = dbuf("v_hbm", [T_all, Dm], BF16, 128, OBW)
    r_hbm = dbuf("r_hbm", [T_own, Dm], BF16, 128, OBW)
    g_hbm = dbuf("g_hbm", [T_own, Dm], BF16, 128, OBW)
    xn_hbm = dbuf("xn_hbm", [T_own, Dm], F32, 128, 128)
    y_hbm = dbuf("y_hbm", [T_own, Dm], BF16, 128, Dm)
    x2_hbm = dbuf("x2_hbm", [T_own, Dm], F32, 128, OBW)
    ln2_hbm = dbuf("ln2_hbm", [T_own, Dm], BF16, 128, Dm)
    h_hbm = dbuf("h_hbm", [T_own, F], BF16, 128, 512)
    wbf = {w: dbuf(w + "bf", [Dm, Dm], BF16, Dm, Dm)
           for w in ("wk", "wv", "wr", "wg", "wo")}
    FH = min(F, 2048)
    DH = min(Dm, 2048)
    w1bf = dbuf("w1bf", [Dm, F], BF16, 64, FH)
    w3bf = dbuf("w3bf", [Dm, F], BF16, 64, FH)
    w2bf = dbuf("w2bf", [F, Dm], BF16, 64, DH)

    PH = cfg.get("phases", {"cast", "deq", "bln", "bproj", "wkv", "wkv2",
                            "wkv3", "wkv4", "wkvA4", "wkvAE", "wkvEv", "gn",
                            "op", "ln2", "ffa", "ffw2"})
    p = Plan()
    import contextlib
    g_es = contextlib.ExitStack()

    def mk_sbuf(stk):
        def sbuf(name, shape, dt):
            return stk.enter_context(nc.sbuf_tensor(name, list(shape), dt))

        def B(name, shape, dt):
            return Buf(sbuf(name, shape, dt)[:], name)
        return sbuf, B

    sbufG, BG = mk_sbuf(g_es)

    mix_sb = {m: BG(m + "_sb", [128, ND], F32) for m in mix_in}
    mix1_sb = {m: BG(m + "1_sb", [128, ND], F32) for m in mix_in}
    tdp_sb = BG("tdp_sb", [128, P2], F32)
    tfp_sb = BG("tfp_sb", [128, P2], F32)
    E_sb = BG("E_sb", [128, P2], F32)
    tdC_sb = BG("tdC_sb", [128, P2], F32)
    ramps_sb = BG("ramps_sb", [128, len(RAMP_NAMES) * 128], F32)
    masks_sb = BG("masks_sb", [128, len(MASK_NAMES) * 128], BF16)
    eps_sb = BG("eps_sb", [128, 1], F32)

    def ramp(nm):
        i = RAMP_NAMES.index(nm)
        return ramps_sb.ap[:, 128 * i:128 * (i + 1)]

    def maskv(nm, nh):
        i = MASK_NAMES.index(nm)
        one = masks_sb.ap[:, 128 * i:128 * (i + 1)]
        return one.unsqueeze(1).to_broadcast([128, nh, 128])

    PS = [Buf(nc.alloc_psum_tensor(f"P{i}", [128, 512], F32)[:], f"P{i}")
          for i in range(7)]
    PT = Buf(nc.alloc_psum_tensor("PT", [128, 1024], BF16)[:], "PT")

    for m in mix_in:
        p.dma("y", lambda E, a=mix_sb[m].ap, b=mix_in[m]: E.dma_start(out=a, in_=b),
              writes=(mix_sb[m],))
    p.dma("y", lambda E: E.dma_start(out=tdp_sb.ap, in_=td_in), writes=(tdp_sb,))
    p.dma("y", lambda E: E.dma_start(out=tfp_sb.ap, in_=tf_in), writes=(tfp_sb,))
    p.dma("y", lambda E: E.dma_start(
        out=ramps_sb.ap.rearrange("p (n c) -> p n c", n=len(RAMP_NAMES)),
        in_=ramps_in.unsqueeze(0).to_broadcast(
            [128, len(RAMP_NAMES), 128])), writes=(ramps_sb,))
    p.dma("y", lambda E: E.dma_start(
        out=masks_sb.ap.rearrange("p (n c) -> p n c", n=len(MASK_NAMES)),
        in_=masks_in.transpose([1, 0, 2])), writes=(masks_sb,))
    for m in mix_in:
        p.cop("v", lambda E, o=mix1_sb[m].ap, i=mix_sb[m].ap: E.tensor_scalar(
            o, i, 1.0, -1.0, ALU.subtract, ALU.mult),
            reads=(mix_sb[m],), writes=(mix1_sb[m],))
    p.cop("v", lambda E: E.memset(eps_sb.ap, 1e-5), writes=(eps_sb,))
    p.cop("s", lambda E: E.activation(E_sb.ap, tdp_sb.ap, AF.Exp),
          reads=(tdp_sb,), writes=(E_sb,))
    p.cop("s", lambda E: E.activation(tdC_sb.ap, E_sb.ap, AF.Exp, scale=-float(C)),
          reads=(E_sb,), writes=(tdC_sb,))

    for w in ("wk", "wv", "wr", "wg", "wo") if "cast" in PH else ():
        p.dma("g", lambda E, o=wbf[w].ap, i=wT[w]: E.dma_start(out=o, in_=i),
              writes=wbf[w].rng(0, Dm))

    with contextlib.ExitStack() as ph:
        sbufP, BP = mk_sbuf(ph)
        MH = max(FH, DH)
        NPAR = 4
        dq_pk = [BP(f"dq_pk{i}", [128, MH], I32) for i in range(NPAR)]
        dq_ms = [BP(f"dq_ms{i}", [128, MH], I32) for i in range(NPAR)]
        dq_ls = [BP(f"dq_ls{i}", [128, MH], I32) for i in range(NPAR)]
        dq_sc = [BP(f"dq_sc{i}", [128, 2 * MH], F32) for i in range(NPAR)]
        dq_om = [BP(f"dq_om{i}", [128, MH], BF16) for i in range(NPAR)]
        dq_ol = [BP(f"dq_ol{i}", [128, MH], BF16) for i in range(NPAR)]
        dq_i = [0]

        def dequant(pk_in, s_in, out_hbm, NQ, W, WHmax):
            nchunks = (W + WHmax - 1) // WHmax
            for qp in range(NQ // 2):
                for hc in range(nchunks):
                    hs = hc * WHmax
                    WH = min(WHmax, W - hs)
                    par = dq_i[0] % NPAR
                    dq_i[0] += 1
                    pkb, scb = dq_pk[par], dq_sc[par]
                    msb_, lsb_ = dq_ms[par], dq_ls[par]
                    omb, olb = dq_om[par], dq_ol[par]
                    pk = pkb.ap[:, 0:WH]
                    for half in range(2):
                        p.dma("g", lambda E, o=pk, i=pk_in, q=qp, h=hs,
                              hf=half, W_=WH: E.dma_start(
                            out=o[64 * hf:64 * hf + 64, :],
                            in_=i[:, 2 * q + hf, h:h + W_]),
                            writes=(pkb,))
                    sc = scb.ap[:, 0:2 * WH].rearrange("p (b c) -> p b c", b=2)
                    for half in range(2):
                        p.dma("s", lambda E, o=sc[64 * half:64 * half + 64],
                              i=s_in, q=qp, h=hs, hf=half, W_=WH: E.dma_start(
                            out=o,
                            in_=i[2 * q + hf:2 * q + hf + 1, :, h:h + W_]
                                .to_broadcast([64, 2, W_])),
                            writes=(scb,))
                    ms, ls = msb_.ap[:, 0:WH], lsb_.ap[:, 0:WH]
                    p.cop("v", lambda E, o=ms, i=pk: E.tensor_single_scalar(
                        o, i, 4, ALU.arith_shift_right),
                        reads=(pkb,), writes=(msb_,))
                    p.cop("v", lambda E, o=ls, i=pk: E.tensor_scalar(
                        o, i, 15, 8, ALU.bitwise_and, ALU.bitwise_xor),
                        reads=(pkb,), writes=(lsb_,))
                    o0, o1 = omb.ap[:, 0:WH], olb.ap[:, 0:WH]
                    p.cop("v", lambda E, o=o0, i=ms, s=sc[:, 0]:
                          E.scalar_tensor_tensor(o, i, 0.0, s, ALU.add,
                                                 ALU.mult),
                          reads=(msb_, scb), writes=(omb,))
                    p.cop("v", lambda E, o=o1, i=ls, s=sc[:, 1]:
                          E.scalar_tensor_tensor(o, i, -8.0, s, ALU.add,
                                                 ALU.mult),
                          reads=(lsb_, scb), writes=(olb,))
                    for half, bsrc in ((0, omb), (1, olb)):
                        for sub in range(2):
                            r0 = 128 * (2 * qp + sub) + 64 * half
                            p.dma("y", lambda E, b=bsrc, r=r0, h=hs, sb_=sub,
                                  W_=WH: E.dma_start(
                                out=out_hbm.ap[r:r + 64, h:h + W_],
                                in_=b.ap[64 * sb_:64 * sb_ + 64, 0:W_]),
                                reads=(bsrc,),
                                writes=out_hbm.rng(r0, r0 + 64, hs, hs + WH))

        if "deq" in PH:
            dequant(w1p, s1d, w1bf, NQ1, F, FH)
            dequant(w3p, s3d, w3bf, NQ1, F, FH)
            dequant(w2p, s2d, w2bf, NQ2, Dm, DH)
        p.barrier()

    inv_d = 1.0 / Dm

    def emit_ln(src_ap, src_row, dst_hbm, dst_row, bufs, src_d=None, n=None):
        xa, st, lst, sq = bufs
        rd = () if src_d is None else tuple(src_d.rng(src_row, src_row + 128))
        p.dma("y", lambda E, o=xa.ap, i=src_ap, t=src_row: E.dma_start(
            out=o, in_=i[t:t + 128, :]), reads=rd, writes=(xa,))
        sums, sumsq = st.ap[:, 0:1], st.ap[:, 1:2]
        mean, varv = st.ap[:, 2:3], st.ap[:, 3:4]
        sd, rstd, m2 = st.ap[:, 4:5], st.ap[:, 5:6], st.ap[:, 6:7]
        p.cop("v", lambda E, o=sums, i=xa.ap: E.reduce_sum(o, i, AX),
              reads=(xa,), writes=(st,))
        p.cop("s", lambda E, o=sq.ap, i=xa.ap, a=sumsq: E.activation(
            o, i, AF.Square, accum_out=a), reads=(xa,), writes=(sq, st))
        p.cop("v", lambda E, o=mean, i=sums: E.tensor_single_scalar(
            o, i, inv_d, ALU.mult), reads=(st,), writes=(st,))
        p.cop("v", lambda E, o=m2, i=mean: E.tensor_tensor(
            o, i, i, ALU.mult), reads=(st,), writes=(st,))
        p.cop("v", lambda E, o=varv, i=sumsq: E.tensor_single_scalar(
            o, i, inv_d, ALU.mult), reads=(st,), writes=(st,))
        p.cop("v", lambda E, o=varv, a=varv, b=m2: E.tensor_sub(o, a, b),
              reads=(st,), writes=(st,))
        p.cop("s", lambda E, o=sd, i=varv: E.activation(
            o, i, AF.Sqrt, bias=eps_sb.ap), reads=(st, eps_sb), writes=(st,))
        p.cop("v", lambda E, o=rstd, i=sd: E.reciprocal(o, i),
              reads=(st,), writes=(st,))
        p.cop("v", lambda E, o=lst.ap, i=xa.ap, a=mean, b=rstd: E.tensor_scalar(
            o, i, a, b, ALU.subtract, ALU.mult), reads=(xa, st), writes=(lst,))
        p.dma("y", lambda E, o=dst_hbm.ap, r=dst_row, i=lst.ap: E.dma_start(
            out=o[r:r + 128, :], in_=i), reads=(lst,),
            writes=dst_hbm.rng(dst_row, dst_row + 128))

    with contextlib.ExitStack() as ph:
        sbufP, BP = mk_sbuf(ph)
        xa_sb = [BP(f"xa{i}", [128, Dm], F32) for i in range(2)]
        sq_sb = [BP(f"sq{i}", [128, Dm], F32) for i in range(2)]
        stat_sb = [BP(f"stt{i}", [128, 8], F32) for i in range(2)]
        ln_st = [BP(f"lnst{i}", [128, Dm], BF16) for i in range(2)]
        p.cop("v", lambda E: E.memset(ln_st[0].ap, 0.0), writes=(ln_st[0],))
        p.dma("y", lambda E: E.dma_start(out=ln1_hbm.ap[0:128, :],
                                         in_=ln_st[0].ap),
              reads=(ln_st[0],), writes=ln1_hbm.rng(0, 128))
        for tt in range(T_all // 128) if "bln" in PH else ():
            par = tt % 2
            emit_ln(x_in, tt * 128, ln1_hbm, 128 + tt * 128,
                    (xa_sb[par], stat_sb[par], ln_st[par], sq_sb[par]))
        p.barrier()

    with contextlib.ExitStack() as ph:
        sbufP, BP = mk_sbuf(ph)
        lnT_t = sbufP("lnT_sb", [128, ND * (TB + 128)], BF16)
        lnT_v = lnT_t[:].rearrange("p (n c) -> p n c", n=ND)
        lnT_sb = [Buf(lnT_v[:, d, :], f"lnT{d}") for d in range(ND)]
        mixT_t = [sbufP(f"mixT{i}", [128, ND * TB], BF16) for i in range(2)]
        mixT_sb = [[Buf(mixT_t[i][:].rearrange("p (n c) -> p n c", n=ND)[:, d, :],
                        f"mixT{i}_{d}") for d in range(ND)] for i in range(2)]
        mixtmp = [BP(f"mixtmp{i}", [128, TB], BF16) for i in range(2)]
        wslab_t = [sbufP(f"wslab{i}", [128, ND * OBW], BF16) for i in range(2)]
        wslab_sb = [Buf(wslab_t[i][:].rearrange("p (n c) -> p n c", n=ND),
                        f"wsl{i}") for i in range(2)]
        ev_sb = [BP(f"ev{i}", [128, OBW], BF16) for i in range(2)]

        def fetch_lnT(t0):
            for d in range(ND):
                p.dma("s", lambda E, o=lnT_sb[d].ap, dd=d, t=t0:
                      E.dma_start_transpose(
                    o, ln1_hbm.ap[t:t + TB + 128, 128 * dd:128 * dd + 128]),
                    reads=tuple(ln1_hbm.rng(t0, t0 + TB + 128)),
                    writes=(lnT_sb[d],))

        def build_mixT(mname, parity):
            mx = mixT_sb[parity]
            for d in range(ND):
                cur = lnT_sb[d].ap[:, 128:128 + TB]
                sh = lnT_sb[d].ap[:, 127:127 + TB]
                tmp = mixtmp[d % 2]
                p.cop("v", lambda E, o=tmp.ap[:, 0:TB], i=sh,
                      s=mix1_sb[mname].ap[:, d:d + 1]: E.tensor_scalar(
                      o, i, s, None, ALU.mult),
                      reads=(lnT_sb[d], mix1_sb[mname]), writes=(tmp,))
                p.cop("v", lambda E, o=mx[d].ap, i=cur,
                      s=mix_sb[mname].ap[:, d:d + 1], t=tmp.ap[:, 0:TB]:
                      E.scalar_tensor_tensor(o, i, s, t, ALU.mult, ALU.add),
                      reads=(lnT_sb[d], mix_sb[mname], tmp), writes=(mx[d],))
            return mx

        def mm_sweep(stat_of, slab, bank, nk, obw, tpos=None):
            for kk in range(nk):
                st_buf, st_ap = stat_of(kk)
                sl_ap = slab.ap[:, kk, 0:obw]
                if kk == 0:
                    p.cop("t", lambda E, o=bank.ap[:, 0:obw], a=st_ap, b=sl_ap:
                          E.matmul(o, a, b, start=True, stop=(nk == 1)),
                        reads=(st_buf, slab), writes=(bank,))
                else:
                    p.cop("t", lambda E, o=bank.ap[:, 0:obw], a=st_ap, b=sl_ap,
                          sp=(kk == nk - 1): E.matmul(o, a, b, start=False,
                                                      stop=sp),
                        reads=(st_buf, slab), writes=(bank,), waw_free=True)

        PROJ = [("wk", "mk", k_hbm, False), ("wv", "mv", v_hbm, False),
                ("wr", "mr", r_hbm, True), ("wg", "mg", g_hbm, True)]
        for blk in range(NB) if "bproj" in PH else ():
            t0 = blk * TB
            fetch_lnT(t0)
            for pi, (w, m, dst, own) in enumerate(PROJ):
                if own and blk < HALO // TB:
                    continue
                mx = build_mixT(m, pi % 2)
                for ob in range(NO):
                    slab = wslab_sb[ob % 2]
                    p.dma("y", lambda E, o=slab.ap[:, :, 0:OBW], i=wbf[w].ap,
                          b=ob: E.dma_start(
                        out=o, in_=i.rearrange("(n p) c -> p n c", p=128)
                        [:, :, OBW * b:OBW * b + OBW]),
                        reads=tuple(wbf[w].rng(0, Dm)), writes=(slab,))
                    for ts in range(NT):
                        bank = PS[ts % 2]
                        mm_sweep(lambda kk: (mx[kk],
                                             mx[kk].ap[:, 128 * ts:128 * ts + 128]),
                                 slab, bank, ND, OBW)
                        evb = ev_sb[ts % 2]
                        fn = AF.Silu if w == "wg" else AF.Copy
                        p.cop("s", lambda E, o=evb.ap[:, 0:OBW],
                              i=bank.ap[:, 0:OBW], f=fn: E.activation(o, i, f),
                              reads=(bank,), writes=(evb,))
                        dst_t = t0 + ts * 128 - (HALO if own else 0)
                        p.dma("y", lambda E, o=dst.ap, t=dst_t, b=ob,
                              i=evb.ap[:, 0:OBW]: E.dma_start(
                            out=o[t:t + 128, OBW * b:OBW * b + OBW], in_=i),
                            reads=(evb,),
                            writes=dst.rng(dst_t, dst_t + 128, ob * OBW,
                                           ob * OBW + OBW))
        p.barrier()

    NMAT = len(RAMP_NAMES)
    KKINDS = ["k4c", "kL4", "k8", "k16", "k32", "k64"]
    RKINDS = ["r4c", "rL4", "r8", "r16", "r32", "r64"]
    NKK = len(KKINDS)
    NRK = len(RKINDS)
    import contextlib as _ctx
    with _ctx.ExitStack() as ph:
        sbufP, BP = mk_sbuf(ph)
        kf = [BP(f"kf{s}", [128, T_all], BF16) for s in range(2)]
        rf = [BP(f"rf{s}", [128, T_own], BF16) for s in range(2)]
        kt = [BP(f"kt{s}", [128, T_all], BF16) for s in range(2)]
        vt = [BP(f"vt{s}", [128, T_all], BF16) for s in range(2)]
        mats = [BP(f"mats{s}", [128, NMAT * 128], BF16) for s in range(2)]
        matz = [[BP(f"matz{s}_{hh}", [128, (NKK + 1) * 128], BF16)
                 for hh in range(2)] for s in range(2)]
        skct = [BP(f"skct{s}", [128, 128], BF16) for s in range(2)]
        matscr = [BP(f"matscr{s}", [128, 128], F32) for s in range(2)]
        idn_sb = BP("idn_sb", [128, 128], BF16)
        hm_sb = BP("hm_sb", [128, 2], F32)
        tfz = [BP(f"tfz{s}", [128, 2], F32) for s in range(2)]
        ksc = [[BP(f"ksc{s}_{hh}", [128, NKK * T_own], BF16) for hh in range(2)]
               for s in range(2)]
        rsc = [BP(f"rsc{s}", [128, NRK * T_own], BF16) for s in range(2)]
        rsiz = [[BP(f"rsiz{s}_{hh}", [128, T_own], BF16) for hh in range(2)]
                for s in range(2)]
        kut = [BP(f"kut{s}", [128, T_all], BF16) for s in range(2)]
        ktfz = [[BP(f"ktfz{s}_{hh}", [128, T_own], BF16) for hh in range(2)]
                for s in range(2)]
        Ssb = [BP(f"S{s}", [128, 64], F32) for s in range(2)]
        Sb16 = [BP(f"Sb{s}", [128, 64], BF16) for s in range(2)]
        stg = [BP(f"stg{i}", [128, 512], BF16) for i in range(5)]
        xnst = BP("xnst", [128, 256], F32)

        def matv(s, nm):
            i = RAMP_NAMES.index(nm)
            return mats[s].ap[:, 128 * i:128 * (i + 1)]

        def matzv(s, hh, nm):
            i = (KKINDS + ["ri"]).index(nm)
            return matz[s][hh].ap[:, 128 * i:128 * (i + 1)]

        p.cop("v", lambda E: E.tensor_copy(
            idn_sb.ap, masks_sb.ap[:, 128 * MASK_NAMES.index("mI"):
                                   128 * (MASK_NAMES.index("mI") + 1)]),
            reads=(masks_sb,), writes=(idn_sb,))
        p.cop("v", lambda E: E.memset(hm_sb.ap, 0.0), writes=(hm_sb,))
        p.cop("v", lambda E: E.memset(hm_sb.ap[0:64, 0:1], 1.0),
              writes=(hm_sb,))
        p.cop("v", lambda E: E.memset(hm_sb.ap[64:128, 1:2], 1.0),
              writes=(hm_sb,))

        psA, psB_, psC, psD, psE = PS[0], PS[1], PS[2], PS[3], PS[4]
        psO, psU = PS[5], PS[6]

        for grp in range(NG) if "wkv" in PH else ():
            for s in range(2):
                pr = 2 * grp + s
                p.dma("s", lambda E, o=kf[s].ap, q=pr: E.dma_start_transpose(
                    o, k_hbm.ap[:, 128 * q:128 * q + 128]),
                    reads=tuple(k_hbm.rng(0, T_all, 128 * pr, 128 * pr + 128)),
                    writes=(kf[s],))
                p.dma("s", lambda E, o=rf[s].ap, q=pr: E.dma_start_transpose(
                    o, r_hbm.ap[:, 128 * q:128 * q + 128]),
                    reads=tuple(r_hbm.rng(0, T_own, 128 * pr, 128 * pr + 128)),
                    writes=(rf[s],))
                p.dma("y", lambda E, o=kt[s].ap.rearrange(
                    "p (n c) -> p n c", n=NCH), q=pr: E.dma_start(
                    out=o, in_=k_hbm.ap.rearrange("(n p) c -> p n c", p=128)
                    [:, :, 128 * q:128 * q + 128]),
                    reads=tuple(k_hbm.rng(0, T_all, 128 * pr, 128 * pr + 128)),
                    writes=(kt[s],))
                p.dma("y", lambda E, o=vt[s].ap.rearrange(
                    "p (n c) -> p n c", n=NCH), q=pr: E.dma_start(
                    out=o, in_=v_hbm.ap.rearrange("(n p) c -> p n c", p=128)
                    [:, :, 128 * q:128 * q + 128]),
                    reads=tuple(v_hbm.rng(0, T_all, 128 * pr, 128 * pr + 128)),
                    writes=(vt[s],))
                for nm in RAMP_NAMES:
                    p.cop("v", lambda E, o=matscr[s].ap, r=ramp(nm),
                          e=E_sb.ap[:, pr:pr + 1]: E.tensor_scalar(
                          o, r, e, None, ALU.mult),
                          reads=(ramps_sb, E_sb), writes=(matscr[s],))
                    p.cop("s", lambda E, o=matv(s, nm), i=matscr[s].ap:
                          E.activation(o, i, AF.Exp, scale=-1.0),
                          reads=(matscr[s],), writes=(mats[s],))
                for hh in range(2):
                    for nm in KKINDS + ["ri"]:
                        p.cop("v", lambda E, o=matzv(s, hh, nm),
                              i=matv(s, nm), m=hm_sb.ap[:, hh:hh + 1]:
                              E.tensor_scalar(o, i, m, None, ALU.mult),
                              reads=(mats[s], hm_sb), writes=(matz[s][hh],))
                p.cop("v", lambda E, o=tfz[s].ap,
                      i=tfp_sb.ap[:, pr:pr + 1].to_broadcast([128, 2]),
                      m=hm_sb.ap: E.tensor_tensor(o, i, m, ALU.mult),
                      reads=(tfp_sb, hm_sb), writes=(tfz[s],))
                p.cop("t", lambda E, o=PT.ap[:, 0:128], i=matv(s, "ku"):
                      E.transpose(o, i, idn_sb.ap),
                      reads=(mats[s], idn_sb), writes=(PT,))
                p.cop("v", lambda E, o=skct[s].ap, i=PT.ap[:, 0:128]:
                      E.tensor_copy(o, i), reads=(PT,), writes=(skct[s],))
                nto = T_own // 128
                for hh in range(2):
                    for ki in range(NKK):
                        nm = KKINDS[ki]
                        p.cop("g", lambda E,
                              o=ksc[s][hh].ap[:, ki * T_own:(ki + 1) * T_own]
                              .rearrange("p (n c) -> p n c", c=128),
                              i=kf[s].ap[:, HALO:T_all]
                              .rearrange("p (n c) -> p n c", c=128),
                              m=matzv(s, hh, nm).unsqueeze(1)
                              .to_broadcast([128, nto, 128]):
                              E.tensor_tensor(o, i, m, ALU.mult),
                              reads=(kf[s], matz[s][hh]), writes=(ksc[s][hh],))
                    p.cop("g", lambda E,
                          o=rsiz[s][hh].ap.rearrange("p (n c) -> p n c", c=128),
                          i=rf[s].ap.rearrange("p (n c) -> p n c", c=128),
                          m=matzv(s, hh, "ri").unsqueeze(1)
                          .to_broadcast([128, nto, 128]):
                          E.tensor_tensor(o, i, m, ALU.mult),
                          reads=(rf[s], matz[s][hh]), writes=(rsiz[s][hh],))
                    p.cop("v", lambda E, o=ktfz[s][hh].ap,
                          i=kf[s].ap[:, HALO:T_all],
                          t=tfz[s].ap[:, hh:hh + 1]: E.tensor_scalar(
                          o, i, t, None, ALU.mult),
                          reads=(kf[s], tfz[s]), writes=(ktfz[s][hh],))
                for ri_ in range(NRK):
                    nm = RKINDS[ri_]
                    p.cop("g", lambda E,
                          o=rsc[s].ap[:, ri_ * T_own:(ri_ + 1) * T_own]
                          .rearrange("p (n c) -> p n c", c=128),
                          i=rf[s].ap.rearrange("p (n c) -> p n c", c=128),
                          m=matv(s, nm).unsqueeze(1)
                          .to_broadcast([128, nto, 128]):
                          E.tensor_tensor(o, i, m, ALU.mult),
                          reads=(rf[s], mats[s]), writes=(rsc[s],))
                p.cop("g", lambda E,
                      o=kut[s].ap.rearrange("p (n c) -> p n c", c=128),
                      i=kt[s].ap.rearrange("p (n c) -> p n c", c=128),
                      m=skct[s].ap.unsqueeze(1).to_broadcast([128, NCH, 128]):
                      E.tensor_tensor(o, i, m, ALU.mult),
                      reads=(kt[s], skct[s]), writes=(kut[s],))
                p.cop("v", lambda E, o=Ssb[s].ap: E.memset(o, 0.0),
                      writes=(Ssb[s],))
                p.cop("v", lambda E, o=Sb16[s].ap: E.memset(o, 0.0),
                      writes=(Sb16[s],))

            def ksz(s, hh, nm, j0, jn):
                ki = KKINDS.index(nm)
                return ksc[s][hh].ap[:, ki * T_own + j0:ki * T_own + j0 + jn]

            def rsp(s, nm, i0, in_):
                ri_ = RKINDS.index(nm)
                return rsc[s].ap[:, ri_ * T_own + i0:ri_ * T_own + i0 + in_]

            for ch in (range(NCH) if "wkv2" in PH else ()):
                tok0 = ch * C
                owned = ch >= NCH_H and "wkv3" in PH
                oc0 = tok0 - HALO
                if owned:
                    p.cop("v", lambda E: E.memset(psE.ap, 0.0), writes=(psE,))
                    args = {psA.name: [], psB_.name: [], psC.name: [],
                            psD.name: [], psE.name: []}
                    bankof = {b.name: b for b in (psA, psB_, psC, psD, psE)}
                    def amm(bank, stat_ap, mov_ap, rbufs, region, bp=0):
                        args[bank.name].append((stat_ap, mov_ap, rbufs, region,
                                                bp))
                    for s in range(2):
                        for hh in range(2):
                            q = 2 * s + hh
                            qc = slice(128 * q, 128 * q + 128)
                            kb = ksc[s][hh]
                            vb = (kb, rsc[s])
                            amm(psA, ksz(s, hh, "k4c", oc0, 128),
                                rsp(s, "r4c", oc0, 128), vb, psA.ap[:, qc])
                            amm(psB_, ksz(s, hh, "kL4", oc0, 128),
                                rsp(s, "rL4", oc0, 128), vb, psB_.ap[:, qc])
                            amm(psC, ksz(s, hh, "k8", oc0, 128),
                                rsp(s, "r8", oc0, 128), vb, psC.ap[:, qc])
                            amm(psD, ktfz[s][hh].ap[:, oc0:oc0 + 128],
                                rf[s].ap[:, oc0:oc0 + 128],
                                (ktfz[s][hh], rf[s]), psD.ap[:, qc])
                            amm(psE, ksz(s, hh, "k64", oc0, 64),
                                rsp(s, "r64", oc0 + 64, 64), vb,
                                psE.ap[0:64, 128 * q + 64:128 * q + 128])
                            for wloc in range(2):
                                j0 = 64 * wloc
                                amm(psE, ksz(s, hh, "k32", oc0 + j0, 32),
                                    rsp(s, "r32", oc0 + j0 + 32, 32), vb,
                                    psE.ap[j0:j0 + 32,
                                           128 * q + j0 + 32:128 * q + j0 + 64],
                                    j0)
                            for wloc in range(4):
                                j0 = 32 * wloc
                                amm(psE, ksz(s, hh, "k16", oc0 + j0, 16),
                                    rsp(s, "r16", oc0 + j0 + 16, 16), vb,
                                    psE.ap[j0:j0 + 16,
                                           128 * q + j0 + 16:128 * q + j0 + 32],
                                    j0)
                    for bname, mms in args.items():
                        bank = bankof[bname]
                        ise = bname == psE.name
                        if ise and "wkvAE" not in PH:
                            continue
                        if not ise and "wkvA4" not in PH:
                            continue
                        for mi, (stat_ap, mov_ap, rbufs, region, bp) in \
                                enumerate(mms):
                            st = (mi == 0) and not ise
                            sp = (mi == len(mms) - 1) and not ise
                            tp = (0, bp) if bp else None
                            p.cop("t", lambda E, o=region, a=stat_ap, b=mov_ap,
                                  st_=st, sp_=sp, sk=ise, tp_=tp: E.matmul(
                                  o, a, b, start=st_, stop=sp_,
                                  skip_group_check=sk, tile_position=tp_),
                                  reads=rbufs, writes=(bank,),
                                  waw_free=(mi != 0))
                    evlist = []
                    if "wkvA4" in PH and "wkvEv" in PH:
                        evlist += [(0, psA, "m4"), (1, psB_, "m8"),
                                   (2, psC, "m16"), (3, psD, "mI")]
                    if "wkvAE" in PH and "wkvEv" in PH:
                        evlist += [(4, psE, "mB")]
                    for stgi, bank, mnm in evlist:
                        p.cop("v", lambda E,
                              o=stg[stgi].ap.rearrange("p (n c) -> p n c", c=128),
                              i=bank.ap.rearrange("p (n c) -> p n c", c=128),
                              m=maskv(mnm, 4): E.tensor_tensor(
                              o, i, m, ALU.mult),
                              reads=(bank, masks_sb), writes=(stg[stgi],))
                    omms = [] if "wkv4" in PH else None
                    for s in (range(2) if omms is not None else ()):
                        for hh in range(2):
                            q = 2 * s + hh
                            oreg = psO.ap[:, 64 * q:64 * q + 64]
                            vmov = vt[s].ap[:, tok0 + 64 * hh:tok0 + 64 * hh + 64]
                            for stgi in range(5):
                                omms.append((stg[stgi].ap[:, 128 * q:128 * q + 128],
                                             vmov, (stg[stgi], vt[s]), oreg))
                            omms.append((rsiz[s][hh].ap[:, oc0:oc0 + 128],
                                         Sb16[s].ap,
                                         (rsiz[s][hh], Sb16[s]), oreg))
                    for mi, (stat_ap, mov_ap, rbufs, region) in \
                            enumerate(omms or ()):
                        p.cop("t", lambda E, o=region, a=stat_ap, b=mov_ap,
                              st_=(mi == 0), sp_=(mi == len(omms) - 1):
                              E.matmul(o, a, b, start=st_, stop=sp_),
                              reads=rbufs, writes=(psO,), waw_free=(mi != 0))
                p.cop("v", lambda E: E.memset(psU.ap[:, 0:128], 0.0),
                      writes=(psU,))
                for s in range(2):
                    for hh in range(2):
                        h0 = 64 * hh
                        p.cop("t", lambda E,
                              o=psU.ap[h0:h0 + 64, 64 * s:64 * s + 64],
                              a=kut[s].ap[:, tok0 + h0:tok0 + h0 + 64],
                              b=vt[s].ap[:, tok0 + h0:tok0 + h0 + 64],
                              tp=(0, h0): E.matmul(
                              o, a, b, start=False, stop=False,
                              tile_position=tp, skip_group_check=True),
                              reads=(kut[s], vt[s]), writes=(psU,),
                              waw_free=not (s == 0 and hh == 0))
                for s in range(2):
                    pr = 2 * grp + s
                    p.cop("v", lambda E, o=Ssb[s].ap, i=Ssb[s].ap,
                          c=tdC_sb.ap[:, pr:pr + 1],
                          u=psU.ap[:, 64 * s:64 * s + 64]:
                          E.scalar_tensor_tensor(o, i, c, u, ALU.mult, ALU.add),
                          reads=(Ssb[s], tdC_sb, psU), writes=(Ssb[s],))
                    p.cop("v", lambda E, o=Sb16[s].ap, i=Ssb[s].ap:
                          E.tensor_copy(o, i), reads=(Ssb[s],), writes=(Sb16[s],))
                if owned and "wkv4" in PH:
                    p.cop("s", lambda E, o=xnst.ap, i=psO.ap[:, 0:256]:
                          E.activation(o, i, AF.Copy), reads=(psO,),
                          writes=(xnst,))
                    for s in range(2):
                        pr = 2 * grp + s
                        p.dma("y", lambda E, o=xn_hbm.ap, r=oc0, q=pr,
                              i=xnst.ap[:, 128 * s:128 * s + 128]: E.dma_start(
                            out=o[r:r + 128, 128 * q:128 * q + 128], in_=i),
                            reads=(xnst,),
                            writes=xn_hbm.rng(oc0, oc0 + 128, 128 * pr,
                                              128 * pr + 128))
        p.barrier()

    inv64 = 1.0 / 64.0
    with _ctx.ExitStack() as ph:
        sbufP, BP = mk_sbuf(ph)
        xng = [BP(f"xng{i}", [128, Dm], F32) for i in range(2)]
        gg = [BP(f"gg{i}", [128, Dm], BF16) for i in range(2)]
        sqg = BP("sqg", [128, Dm], F32)
        stg2 = [BP(f"stg2{i}", [128, 8 * H], F32) for i in range(2)]
        yst = [BP(f"yst{i}", [128, Dm], BF16) for i in range(2)]
        gnw_bc = BP("gnw_bc", [128, Dm], F32)
        gnb_bc = BP("gnb_bc", [128, Dm], F32)
        p.dma("y", lambda E: E.dma_start(
            out=gnw_bc.ap, in_=gnw_in.to_broadcast([128, Dm])),
            writes=(gnw_bc,))
        p.dma("y", lambda E: E.dma_start(
            out=gnb_bc.ap, in_=gnb_in.to_broadcast([128, Dm])),
            writes=(gnb_bc,))
        for tt in range(T_own // 128) if "gn" in PH else ():
            par = tt % 2
            xa, st, yb, gb = xng[par], stg2[par], yst[par], gg[par]
            t0 = tt * 128
            p.dma("y", lambda E, o=xa.ap, t=t0: E.dma_start(
                out=o, in_=xn_hbm.ap[t:t + 128, :]),
                reads=tuple(xn_hbm.rng(t0, t0 + 128)), writes=(xa,))
            p.dma("y", lambda E, o=gb.ap, t=t0: E.dma_start(
                out=o, in_=g_hbm.ap[t:t + 128, :]),
                reads=tuple(g_hbm.rng(t0, t0 + 128)), writes=(gb,))
            sums = st.ap[:, 0:H]
            sumsq = st.ap[:, H:2 * H]
            mean = st.ap[:, 2 * H:3 * H]
            varv = st.ap[:, 3 * H:4 * H]
            sd = st.ap[:, 4 * H:5 * H]
            rstd = st.ap[:, 5 * H:6 * H]
            m2 = st.ap[:, 6 * H:7 * H]
            xv = xa.ap.rearrange("p (h e) -> p h e", h=H)
            p.cop("v", lambda E, o=sums, i=xv: E.reduce_sum(o, i, AX),
                  reads=(xa,), writes=(st,))
            p.cop("s", lambda E, o=sqg.ap, i=xa.ap: E.activation(
                o, i, AF.Square), reads=(xa,), writes=(sqg,))
            p.cop("v", lambda E, o=sumsq,
                  i=sqg.ap.rearrange("p (h e) -> p h e", h=H):
                  E.reduce_sum(o, i, AX), reads=(sqg,), writes=(st,))
            p.cop("v", lambda E, o=mean, i=sums: E.tensor_single_scalar(
                o, i, inv64, ALU.mult), reads=(st,), writes=(st,))
            p.cop("v", lambda E, o=m2, i=mean: E.tensor_tensor(
                o, i, i, ALU.mult), reads=(st,), writes=(st,))
            p.cop("v", lambda E, o=varv, i=sumsq: E.tensor_single_scalar(
                o, i, inv64, ALU.mult), reads=(st,), writes=(st,))
            p.cop("v", lambda E, o=varv, a=varv, b=m2: E.tensor_sub(o, a, b),
                  reads=(st,), writes=(st,))
            p.cop("s", lambda E, o=sd, i=varv: E.activation(
                o, i, AF.Sqrt, bias=eps_sb.ap), reads=(st, eps_sb), writes=(st,))
            p.cop("v", lambda E, o=rstd, i=sd: E.reciprocal(o, i),
                  reads=(st,), writes=(st,))
            meanb = mean.unsqueeze(2).to_broadcast([128, H, 64])
            rstdb = rstd.unsqueeze(2).to_broadcast([128, H, 64])
            p.cop("g", lambda E, o=xv, i=xv, m=meanb: E.tensor_sub(o, i, m),
                  reads=(xa, st), writes=(xa,))
            p.cop("v", lambda E, o=xv, i=xv, m=rstdb: E.tensor_tensor(
                o, i, m, ALU.mult), reads=(xa, st), writes=(xa,))
            p.cop("g", lambda E, o=xa.ap, i=xa.ap, m=gnw_bc.ap: E.tensor_tensor(
                o, i, m, ALU.mult), reads=(xa, gnw_bc), writes=(xa,))
            p.cop("g", lambda E, o=xa.ap, i=xa.ap, m=gnb_bc.ap: E.tensor_add(
                o, i, m), reads=(xa, gnb_bc), writes=(xa,))
            p.cop("v", lambda E, o=yb.ap, i=xa.ap, m=gb.ap: E.tensor_tensor(
                o, i, m, ALU.mult), reads=(xa, gb), writes=(yb,))
            p.dma("y", lambda E, o=y_hbm.ap, t=t0, i=yb.ap: E.dma_start(
                out=o[t:t + 128, :], in_=i), reads=(yb,),
                writes=y_hbm.rng(t0, t0 + 128))
        p.barrier()

    with _ctx.ExitStack() as ph:
        sbufP, BP = mk_sbuf(ph)
        yT_t = sbufP("yT_sb", [128, ND * T_own], BF16)
        yT_v = yT_t[:].rearrange("p (n c) -> p n c", n=ND)
        yT_sb = [Buf(yT_v[:, d, :], f"yT{d}") for d in range(ND)]
        woslab_t = [sbufP(f"wosl{i}", [128, ND * OBW], BF16) for i in range(2)]
        woslab = [Buf(woslab_t[i][:].rearrange("p (n c) -> p n c", n=ND),
                      f"wosl{i}") for i in range(2)]
        xp = [BP(f"xp{i}", [128, OBW], F32) for i in range(2)]
        x2st = [BP(f"x2st{i}", [128, OBW], F32) for i in range(2)]
        for d in range(ND) if "op" in PH else ():
            p.dma("s", lambda E, o=yT_sb[d].ap, dd=d: E.dma_start_transpose(
                o, y_hbm.ap[:, 128 * dd:128 * dd + 128]),
                reads=tuple(y_hbm.rng(0, T_own)), writes=(yT_sb[d],))
        for ob in range(NO) if "op" in PH else ():
            slab = woslab[ob % 2]
            p.dma("y", lambda E, o=slab.ap[:, :, 0:OBW], b=ob: E.dma_start(
                out=o, in_=wbf["wo"].ap.rearrange("(n p) c -> p n c", p=128)
                [:, :, OBW * b:OBW * b + OBW]),
                reads=tuple(wbf["wo"].rng(0, Dm)), writes=(slab,))
            for ts in range(T_own // 128):
                bank = PS[ts % 2]
                for kk in range(ND):
                    p.cop("t", lambda E, o=bank.ap[:, 0:OBW],
                          a=yT_sb[kk].ap[:, 128 * ts:128 * ts + 128],
                          b=slab.ap[:, kk, 0:OBW], st=(kk == 0),
                          sp=(kk == ND - 1):
                          E.matmul(o, a, b, start=st, stop=sp),
                          reads=(yT_sb[kk], slab), writes=(bank,),
                          waw_free=(kk != 0))
                xpb = xp[ts % 2]
                x2b = x2st[ts % 2]
                p.dma("y", lambda E, o=xpb.ap, t=ts * 128, b=ob: E.dma_start(
                    out=o, in_=x_in[HALO + t:HALO + t + 128,
                                    OBW * b:OBW * b + OBW]),
                    writes=(xpb,))
                p.cop("v", lambda E, o=x2b.ap, i=bank.ap[:, 0:OBW], m=xpb.ap:
                      E.tensor_add(o, i, m), reads=(bank, xpb), writes=(x2b,))
                p.dma("y", lambda E, o=x2_hbm.ap, t=ts * 128, b=ob, i=x2b.ap:
                      E.dma_start(out=o[t:t + 128, OBW * b:OBW * b + OBW],
                                  in_=i), reads=(x2b,),
                      writes=x2_hbm.rng(ts * 128, ts * 128 + 128, ob * OBW,
                                        ob * OBW + OBW))
        p.barrier()

    with _ctx.ExitStack() as ph:
        sbufP, BP = mk_sbuf(ph)
        xa_sb = [BP(f"xb{i}", [128, Dm], F32) for i in range(2)]
        sq_sb = [BP(f"sqb{i}", [128, Dm], F32) for i in range(2)]
        stat_sb = [BP(f"stb{i}", [128, 8], F32) for i in range(2)]
        ln_st = [BP(f"lnb{i}", [128, Dm], BF16) for i in range(2)]
        for tt in range(T_own // 128) if "ln2" in PH else ():
            par = tt % 2
            emit_ln(x2_hbm.ap, tt * 128, ln2_hbm, tt * 128,
                    (xa_sb[par], stat_sb[par], ln_st[par], sq_sb[par]),
                    src_d=x2_hbm)
        p.barrier()

    NHB = (F + 511) // 512
    with _ctx.ExitStack() as ph:
        sbufP, BP = mk_sbuf(ph)
        ln2T_t = sbufP("ln2T_sb", [128, ND * TB2], BF16)
        ln2T_v = ln2T_t[:].rearrange("p (n c) -> p n c", n=ND)
        ln2T_sb = [Buf(ln2T_v[:, d, :], f"l2T{d}") for d in range(ND)]
        w1sl_t = [sbufP(f"w1sl{i}", [128, ND * 512], BF16) for i in range(2)]
        w1sl = [Buf(w1sl_t[i][:].rearrange("p (n c) -> p n c", n=ND), f"w1sl{i}")
                for i in range(2)]
        w3sl_t = [sbufP(f"w3sl{i}", [128, ND * 512], BF16) for i in range(2)]
        w3sl = [Buf(w3sl_t[i][:].rearrange("p (n c) -> p n c", n=ND), f"w3sl{i}")
                for i in range(2)]
        sast = [BP(f"sast{i}", [128, 512], F32) for i in range(2)]
        hst = [BP(f"hst{i}", [128, 512], BF16) for i in range(2)]
        for fb in range(NFB) if "ffa" in PH else ():
            ft0 = fb * TB2
            for d in range(ND):
                p.dma("s", lambda E, o=ln2T_sb[d].ap, dd=d, t=ft0:
                      E.dma_start_transpose(
                    o, ln2_hbm.ap[t:t + TB2, 128 * dd:128 * dd + 128]),
                    reads=tuple(ln2_hbm.rng(ft0, ft0 + TB2)),
                    writes=(ln2T_sb[d],))
            for hb in range(NHB):
                hw = min(512, F - 512 * hb)
                sl1, sl3 = w1sl[hb % 2], w3sl[hb % 2]
                p.dma("y", lambda E, o=sl1.ap[:, :, 0:hw], b=hb, w_=hw:
                      E.dma_start(
                    out=o, in_=w1bf.ap.rearrange("(n p) c -> p n c", p=128)
                    [:, :, 512 * b:512 * b + w_]),
                    reads=tuple(w1bf.rng(0, Dm, 512 * hb, 512 * hb + hw)),
                    writes=(sl1,))
                p.dma("y", lambda E, o=sl3.ap[:, :, 0:hw], b=hb, w_=hw:
                      E.dma_start(
                    out=o, in_=w3bf.ap.rearrange("(n p) c -> p n c", p=128)
                    [:, :, 512 * b:512 * b + w_]),
                    reads=tuple(w3bf.rng(0, Dm, 512 * hb, 512 * hb + hw)),
                    writes=(sl3,))
                for ts in range(TB2 // 128):
                    bka = PS[2 * (ts % 2)]
                    bkb = PS[2 * (ts % 2) + 1]
                    for kk in range(ND):
                        p.cop("t", lambda E, o=bka.ap[:, 0:hw],
                              a=ln2T_sb[kk].ap[:, 128 * ts:128 * ts + 128],
                              b=sl1.ap[:, kk, 0:hw], st=(kk == 0),
                              sp=(kk == ND - 1):
                              E.matmul(o, a, b, start=st, stop=sp),
                              reads=(ln2T_sb[kk], sl1), writes=(bka,),
                              waw_free=(kk != 0))
                    for kk in range(ND):
                        p.cop("t", lambda E, o=bkb.ap[:, 0:hw],
                              a=ln2T_sb[kk].ap[:, 128 * ts:128 * ts + 128],
                              b=sl3.ap[:, kk, 0:hw], st=(kk == 0),
                              sp=(kk == ND - 1):
                              E.matmul(o, a, b, start=st, stop=sp),
                              reads=(ln2T_sb[kk], sl3), writes=(bkb,),
                              waw_free=(kk != 0))
                    sab = sast[ts % 2]
                    hb_st = hst[ts % 2]
                    p.cop("s", lambda E, o=sab.ap[:, 0:hw], i=bka.ap[:, 0:hw]:
                          E.activation(o, i, AF.Silu), reads=(bka,),
                          writes=(sab,))
                    p.cop("v", lambda E, o=hb_st.ap[:, 0:hw],
                          i=sab.ap[:, 0:hw], m=bkb.ap[:, 0:hw]:
                          E.tensor_tensor(o, i, m, ALU.mult),
                          reads=(sab, bkb), writes=(hb_st,))
                    p.dma("y", lambda E, o=h_hbm.ap, t=ft0 + ts * 128, b=hb,
                          w_=hw, i=hb_st.ap: E.dma_start(
                        out=o[t:t + 128, 512 * b:512 * b + w_],
                        in_=i[:, 0:w_]), reads=(hb_st,),
                        writes=h_hbm.rng(ft0 + ts * 128, ft0 + ts * 128 + 128,
                                         512 * hb, 512 * hb + hw))
        p.barrier()

    with _ctx.ExitStack() as ph:
        sbufP, BP = mk_sbuf(ph)
        hT_t = sbufP("hT_sb", [128, NQ2 * TB2], BF16)
        hT_v = hT_t[:].rearrange("p (n c) -> p n c", n=NQ2)
        hT_sb = [Buf(hT_v[:, d, :], f"hT{d}") for d in range(NQ2)]
        w2sl_t = [sbufP(f"w2sl{i}", [128, NQ2 * OB2], BF16) for i in range(2)]
        w2sl = [Buf(w2sl_t[i][:].rearrange("p (n c) -> p n c", n=NQ2),
                    f"w2sl{i}") for i in range(2)]
        xp2 = [BP(f"xp2{i}", [128, OB2], F32) for i in range(2)]
        outst = [BP(f"outst{i}", [128, OB2], F32) for i in range(2)]
        for fb in range(NFB) if "ffw2" in PH else ():
            ft0 = fb * TB2
            for d in range(NQ2):
                p.dma("s", lambda E, o=hT_sb[d].ap, dd=d, t=ft0:
                      E.dma_start_transpose(
                    o, h_hbm.ap[t:t + TB2, 128 * dd:128 * dd + 128]),
                    reads=tuple(h_hbm.rng(ft0, ft0 + TB2, 128 * d,
                                          128 * d + 128)),
                    writes=(hT_sb[d],))
            for ob in range(NO2):
                slab = w2sl[ob % 2]
                p.dma("y", lambda E, o=slab.ap[:, :, 0:OB2], b=ob: E.dma_start(
                    out=o, in_=w2bf.ap.rearrange("(n p) c -> p n c", p=128)
                    [:, :, OB2 * b:OB2 * b + OB2]),
                    reads=tuple(w2bf.rng(0, F)), writes=(slab,))
                for ts in range(TB2 // 128):
                    bank = PS[ts % 2]
                    for kk in range(NQ2):
                        p.cop("t", lambda E, o=bank.ap[:, 0:OB2],
                              a=hT_sb[kk].ap[:, 128 * ts:128 * ts + 128],
                              b=slab.ap[:, kk, 0:OB2], st=(kk == 0),
                              sp=(kk == NQ2 - 1):
                              E.matmul(o, a, b, start=st, stop=sp),
                              reads=(hT_sb[kk], slab), writes=(bank,),
                              waw_free=(kk != 0))
                    xpb = xp2[ts % 2]
                    osb = outst[ts % 2]
                    t0 = ft0 + ts * 128
                    p.dma("y", lambda E, o=xpb.ap, t=t0, b=ob: E.dma_start(
                        out=o, in_=x2_hbm.ap[t:t + 128, OB2 * b:OB2 * b + OB2]),
                        reads=tuple(x2_hbm.rng(t0, t0 + 128, ob * OB2,
                                               ob * OB2 + OB2)),
                        writes=(xpb,))
                    p.cop("v", lambda E, o=osb.ap, i=bank.ap[:, 0:OB2],
                          m=xpb.ap: E.tensor_add(o, i, m),
                          reads=(bank, xpb), writes=(osb,))
                    p.dma("y", lambda E, o=out_c, t=t0, b=ob, i=osb.ap:
                          E.dma_start(out=o[t:t + 128, OB2 * b:OB2 * b + OB2],
                                      in_=i), reads=(osb,))
        p.barrier()

    p.finish("y")

    stack = emit_all(nc, p)
    return nc



def host_prepare(cfg, inputs):
    n_cores = cfg["n_cores"]
    Dm, F, T_own, HALO = cfg["Dm"], cfg["F"], cfg["T_own"], cfg["HALO"]
    T_all = T_own + HALO
    ND = Dm // 128
    P2 = Dm // 128
    NQ1 = Dm // 128
    NQ2 = F // 128

    x = np.asarray(inputs["L0_attn_hidden"], np.float32).reshape(-1, Dm)
    base = {}
    for nm, w in (("wkT", "L0_attn_w_key"), ("wvT", "L0_attn_w_value"),
                  ("wrT", "L0_attn_w_receptance"), ("wgT", "L0_attn_w_gate"),
                  ("woT", "L0_attn_w_output")):
        base[nm] = np.ascontiguousarray(np.asarray(inputs[w], np.float32).T)
    for nm, m in (("mk", "L0_attn_time_mix_key"), ("mv", "L0_attn_time_mix_value"),
                  ("mr", "L0_attn_time_mix_receptance"),
                  ("mg", "L0_attn_time_mix_gate")):
        base[nm] = np.ascontiguousarray(
            np.asarray(inputs[m], np.float32).reshape(ND, 128).T)
    td = np.asarray(inputs["L0_attn_time_decay"], np.float32)
    tf = np.asarray(inputs["L0_attn_time_faaaa"], np.float32)
    base["td_pair"] = np.ascontiguousarray(td.reshape(P2, 128).T)
    base["tf_pair"] = np.ascontiguousarray(tf.reshape(P2, 128).T)
    base["gnw"] = np.asarray(inputs["L0_attn_gn_weight"], np.float32).reshape(1, Dm)
    base["gnb"] = np.asarray(inputs["L0_attn_gn_bias"], np.float32).reshape(1, Dm)
    base["ramps"] = make_ramps()
    base["masks"] = make_masks()
    for nm, wid, nq in (("1", F, NQ1), ("3", F, NQ1), ("2", Dm, NQ2)):
        w = np.asarray(inputs["L0_ffn_w" + nm], np.int32)
        s = np.asarray(inputs["L0_ffn_s" + nm], np.float32)
        base["w" + nm + "p"] = np.ascontiguousarray(
            w.reshape(wid, nq, 64).transpose(2, 1, 0))
        base["s" + nm + "d"] = np.ascontiguousarray(
            s.reshape(wid, nq, 2).transpose(1, 2, 0))

    in_maps = []
    for c in range(n_cores):
        m = dict(base)
        xc = np.zeros((T_all, Dm), np.float32)
        start = c * T_own - HALO
        lo = max(0, -start)
        xc[lo:] = x[start + lo: c * T_own + T_own]
        m["x_c"] = xc
        in_maps.append(m)
    return in_maps


def kernel(**inputs):
    cfg = full_cfg()
    nc = build_program(cfg)
    in_maps = host_prepare(cfg, inputs)
    res = run_bass_kernel_spmd(nc, in_maps, core_ids=list(range(cfg["n_cores"])))
    outs = [np.asarray(r["out_c"], np.float32) for r in res.results]
    full = np.concatenate(outs, axis=0)
    return full.reshape(1, cfg["n_cores"] * cfg["T_own"], cfg["Dm"])


# version 08 (final)
# speedup vs baseline: 1.6849x; 1.0575x over previous
import numpy as np
import ml_dtypes
import concourse.bass as bass
import concourse.mybir as mybir
from concourse.bass_utils import run_bass_kernel_spmd

F32 = mybir.dt.float32
BF16 = mybir.dt.bfloat16
I32 = mybir.dt.int32
AF = mybir.ActivationFunctionType
ALU = mybir.AluOpType
AX = mybir.AxisListType.X
BF16NP = ml_dtypes.bfloat16


def full_cfg():
    return dict(n_cores=8, Dm=4096, F=11008, T_own=1024, HALO=512, TB=512)



class SemT:
    __slots__ = ("name", "cnt", "h")

    def __init__(self, name):
        self.name, self.cnt, self.h = name, 0, None

    def take(self, n=1):
        self.cnt += n
        return self.cnt


class Rec:
    __slots__ = ("fn", "incs", "tagval")

    def __init__(self, fn):
        self.fn, self.incs, self.tagval = fn, [], None


class Buf:
    __slots__ = ("ap", "w", "readers", "name")

    def __init__(self, ap, name=""):
        self.ap, self.w, self.readers, self.name = ap, None, {}, name


class DBuf:

    def __init__(self, ap, shape, rgran, cgran, name=""):
        self.ap = ap
        self.shape = shape
        self.rg, self.cg = rgran, cgran
        self.name = name
        self.gr = {}

    def _g(self, ri, ci):
        key = (ri, ci)
        b = self.gr.get(key)
        if b is None:
            b = Buf(None, f"{self.name}[{ri},{ci}]")
            self.gr[key] = b
        return b

    def rng(self, r0, r1, c0=0, c1=None):
        if c1 is None:
            c1 = self.shape[1]
        out = []
        for ri in range(r0 // self.rg, (r1 - 1) // self.rg + 1):
            for ci in range(c0 // self.cg, (c1 - 1) // self.cg + 1):
                out.append(self._g(ri, ci))
        return out


QS = ("t", "v", "s", "g", "y")


class Gen:
    __slots__ = ("sem", "end")

    def __init__(self, sem):
        self.sem, self.end = sem, None


class Plan:
    DMA_POOL = 6

    def __init__(self):
        self.q = {k: [] for k in QS}
        self.sems = []
        self.pending = {k: [] for k in QS}
        self.chain = {k: self.new_sem("chain_" + k) for k in ("t", "v", "s", "g")}
        self.waited = {k: {} for k in QS}
        self.dpool = {k: [self.new_sem(f"dma_{k}{i}") for i in range(self.DMA_POOL)]
                      for k in ("y", "s", "g", "v")}
        self.didx = {k: 0 for k in self.dpool}
        self.dopen = {k: None for k in self.dpool}

    def new_sem(self, name):
        s = SemT(name)
        self.sems.append(s)
        return s

    def raw(self, qn, fn):
        rec = Rec(fn)
        self.q[qn].append(rec)
        return rec

    def wait(self, qn, sem, val):
        if val <= 0:
            return
        w = self.waited[qn]
        if w.get(sem.name, 0) >= val:
            return
        w[sem.name] = val
        self.raw(qn, lambda E, s=sem, v=val: E.wait_ge(s.h, v))

    def _resolve(self, dep):
        kind = dep[0]
        if kind == "s":
            return dep[1], dep[2]
        if kind == "d":
            qn, gen = dep[1], dep[2]
            if gen.end is None:
                gen.end = gen.sem.cnt
                if self.dopen[qn] is gen:
                    self.dopen[qn] = None
            return gen.sem, gen.end
        qn, rec = dep[1], dep[2]
        if rec.tagval is None:
            pend = self.pending[qn]
            idx = pend.index(rec)
            ch = self.chain[qn]
            ch.take(idx + 1)
            rec.incs.append((ch, idx + 1))
            for r in pend[: idx + 1]:
                r.tagval = ch.cnt
            del pend[: idx + 1]
        return self.chain[qn], rec.tagval

    def dep_wait(self, qn, dep):
        sem, val = self._resolve(dep)
        self.wait(qn, sem, val)

    def _pre(self, qn, reads, writes, waw_free=False):
        for b in reads:
            if b.w is not None:
                self.dep_wait(qn, b.w)
        for b in writes:
            if not waw_free and b.w is not None:
                self.dep_wait(qn, b.w)
            if not waw_free:
                for d in list(b.readers.values()):
                    self.dep_wait(qn, d)

    def cop(self, qn, fn, reads=(), writes=(), waw_free=False):
        self._pre(qn, reads, writes, waw_free)
        rec = self.raw(qn, fn)
        self.pending[qn].append(rec)
        dep = ("c", qn, rec)
        for b in reads:
            b.readers[qn] = dep
        for b in writes:
            b.w = dep
            b.readers = {}
        return rec

    def dma(self, qn, fn, reads=(), writes=()):
        self._pre(qn, reads, writes)
        gen = self.dopen[qn]
        if gen is None:
            pool = self.dpool[qn]
            sem = pool[self.didx[qn] % len(pool)]
            self.didx[qn] += 1
            if sem.cnt > 0:
                self.wait(qn, sem, sem.cnt)
            gen = Gen(sem)
            self.dopen[qn] = gen
        gen.sem.take(16)
        rec = self.raw(qn, fn)
        rec.incs.append((gen.sem, 16))
        dep = ("d", qn, gen)
        for b in reads:
            b.readers[qn + "dma"] = dep
        for b in writes:
            b.w = dep
            b.readers = {}
        return dep

    def flush_pending(self):
        for qn in ("t", "v", "s", "g"):
            pend = self.pending[qn]
            if pend:
                self._resolve(("c", qn, pend[-1]))

    def barrier(self):
        self.flush_pending()
        for pq in self.dpool:
            g = self.dopen[pq]
            if g is not None:
                g.end = g.sem.cnt
                self.dopen[pq] = None
        targets = [(ch, ch.cnt) for ch in self.chain.values() if ch.cnt]
        for pool in self.dpool.values():
            targets.extend((s, s.cnt) for s in pool if s.cnt)
        for qn in QS:
            for sem, val in targets:
                self.wait(qn, sem, val)

    def finish(self, qn="y"):
        for pq, pool in self.dpool.items():
            g = self.dopen[pq]
            if g is not None:
                g.end = g.sem.cnt
                self.dopen[pq] = None
            for s in pool:
                self.wait(qn, s, s.cnt)
        for k, ch in self.chain.items():
            self.wait(qn, ch, ch.cnt)


def emit_all(nc, plan):
    import contextlib
    stack = contextlib.ExitStack()
    for s in plan.sems:
        s.h = stack.enter_context(nc.semaphore(s.name))
    block = stack.enter_context(nc.Block())

    def runq(E, qn):
        for rec in plan.q[qn]:
            ins = rec.fn(E)
            for sem, n in rec.incs:
                ins.then_inc(sem.h, n)

    @block.sync
    def _(E):
        runq(E, "y")

    @block.vector
    def _(E):
        runq(E, "v")

    @block.scalar
    def _(E):
        runq(E, "s")

    @block.gpsimd
    def _(E):
        runq(E, "g")

    @block.tensor
    def _(E):
        runq(E, "t")

    return stack



RAMP_NAMES = ["r4c", "k4c", "rL4", "kL4", "r8", "k8", "r16", "k16",
              "r32", "k32", "r64", "k64", "ri", "ku"]
BIGR = 1.0e4
CHUNK = 128


def make_ramps():
    i = np.arange(CHUNK).astype(np.float64)
    Z = lambda cond, g: np.where(cond, g, BIGR)
    g = {
        "r4c": (i % 4) - 2.0,
        "k4c": 1.0 - (i % 4),
        "rL4": Z(i % 8 >= 4, (i % 8) - 4.0),
        "kL4": Z(i % 8 < 4, 3.0 - (i % 8)),
        "r8":  Z(i % 16 >= 8, (i % 16) - 8.0),
        "k8":  Z(i % 16 < 8, 7.0 - (i % 16)),
        "r16": Z(i % 32 >= 16, (i % 32) - 16.0),
        "k16": Z(i % 32 < 16, 15.0 - (i % 32)),
        "r32": Z(i % 64 >= 32, (i % 64) - 32.0),
        "k32": Z(i % 64 < 32, 31.0 - (i % 64)),
        "r64": Z(i % 128 >= 64, (i % 128) - 64.0),
        "k64": Z(i % 128 < 64, 63.0 - (i % 128)),
        "ri":  i,
        "ku":  (CHUNK - 1.0) - i,
    }
    return np.stack([g[n] for n in RAMP_NAMES]).astype(np.float32)


MASK_NAMES = ["m4", "m8", "m16", "mI", "mB"]


def make_masks():
    jj = np.arange(CHUNK)[:, None]
    ii = np.arange(CHUNK)[None, :]
    m4 = ((jj // 4 == ii // 4) & (jj < ii)).astype(np.float32)
    m8 = (jj // 8 == ii // 8).astype(np.float32)
    m16 = (jj // 16 == ii // 16).astype(np.float32)
    mI = (jj == ii).astype(np.float32)
    mB = np.zeros((CHUNK, CHUNK))
    for L in (16, 32, 64):
        W = 2 * L
        mB += ((jj // W == ii // W) & (jj % W < L) & (ii % W >= L))
    mB = (mB > 0).astype(np.float32)
    return np.stack([m4, m8, m16, mI, mB]).astype(BF16NP)



def build_program(cfg):
    Dm, F, T_own, HALO, TB = (cfg["Dm"], cfg["F"], cfg["T_own"], cfg["HALO"],
                              cfg["TB"])
    T_all = T_own + HALO
    H = Dm // 64
    P2 = H // 2
    NG = (P2 + 1) // 2
    ND = Dm // 128
    OBW = min(512, Dm)
    NO = Dm // OBW
    C = CHUNK
    NCH = T_all // C
    NCH_H = HALO // C
    NB = T_all // TB
    NT = TB // 128
    NQ1 = Dm // 128
    NQ2 = F // 128
    OB2 = min(256, Dm)
    NO2 = Dm // OB2
    TB2 = min(512, T_own)
    NFB = T_own // TB2
    assert HALO % TB == 0 and T_all % TB == 0 and TB % 128 == 0
    assert T_own % C == 0 and HALO % C == 0 and P2 % 2 == 0

    nc = bass.Bass("TRN2", target_bir_lowering=False, debug=False)

    def ein(name, shape, dt):
        return nc.dram_tensor(name, list(shape), dt, kind="ExternalInput").ap()

    x_in = ein("x_c", [T_all, Dm], F32)
    wT = {w: ein(w + "T", [Dm, Dm], F32) for w in ("wk", "wv", "wr", "wg", "wo")}
    mix_in = {m: ein(m, [128, ND], F32) for m in ("mk", "mv", "mr", "mg")}
    td_in = ein("td_pair", [128, P2], F32)
    tf_in = ein("tf_pair", [128, P2], F32)
    gnw_in = ein("gnw", [1, Dm], F32)
    gnb_in = ein("gnb", [1, Dm], F32)
    ramps_in = ein("ramps", [len(RAMP_NAMES), 128], F32)
    masks_in = ein("masks", [len(MASK_NAMES), 128, 128], BF16)
    w1p = ein("w1p", [64, NQ1, F], I32)
    w3p = ein("w3p", [64, NQ1, F], I32)
    w2p = ein("w2p", [64, NQ2, Dm], I32)
    s1d = ein("s1d", [NQ1, 2, F], F32)
    s3d = ein("s3d", [NQ1, 2, F], F32)
    s2d = ein("s2d", [NQ2, 2, Dm], F32)
    out_c = nc.dram_tensor("out_c", [T_own, Dm], F32, kind="ExternalOutput").ap()

    def dscr(name, shape, dt):
        return nc.dram_tensor(name, list(shape), dt, kind="Internal").ap()

    def dbuf(name, shape, dt, rgran, cgran):
        return DBuf(dscr(name, shape, dt), shape, rgran, cgran, name)

    ln1_hbm = dbuf("ln1_hbm", [T_all + 128, Dm], BF16, 128, Dm)
    k_hbm = dbuf("k_hbm", [T_all, Dm], BF16, 128, OBW)
    v_hbm = dbuf("v_hbm", [T_all, Dm], BF16, 128, OBW)
    r_hbm = dbuf("r_hbm", [T_own, Dm], BF16, 128, OBW)
    g_hbm = dbuf("g_hbm", [T_own, Dm], BF16, 128, OBW)
    xn_hbm = dbuf("xn_hbm", [T_own, Dm], F32, 128, 128)
    y_hbm = dbuf("y_hbm", [T_own, Dm], BF16, 128, Dm)
    x2_hbm = dbuf("x2_hbm", [T_own, Dm], F32, 128, OBW)
    ln2_hbm = dbuf("ln2_hbm", [T_own, Dm], BF16, 128, Dm)
    h_hbm = dbuf("h_hbm", [T_own, F], BF16, 128, 512)
    wbf = {w: dbuf(w + "bf", [Dm, Dm], BF16, Dm, Dm)
           for w in ("wk", "wv", "wr", "wg", "wo")}
    FH = min(F, 2048)
    DH = min(Dm, 2048)
    w1bf = dbuf("w1bf", [Dm, F], BF16, 64, FH)
    w3bf = dbuf("w3bf", [Dm, F], BF16, 64, FH)
    w2bf = dbuf("w2bf", [F, Dm], BF16, 64, DH)

    PH = cfg.get("phases", {"cast", "deq", "bln", "bproj", "wkv", "wkv2",
                            "wkv3", "wkv4", "wkvA4", "wkvAE", "wkvEv", "gn",
                            "op", "ln2", "ffa", "ffw2"})
    p = Plan()
    import contextlib
    g_es = contextlib.ExitStack()

    def mk_sbuf(stk):
        def sbuf(name, shape, dt):
            return stk.enter_context(nc.sbuf_tensor(name, list(shape), dt))

        def B(name, shape, dt):
            return Buf(sbuf(name, shape, dt)[:], name)
        return sbuf, B

    sbufG, BG = mk_sbuf(g_es)

    mix_sb = {m: BG(m + "_sb", [128, ND], F32) for m in mix_in}
    mix1_sb = {m: BG(m + "1_sb", [128, ND], F32) for m in mix_in}
    tdp_sb = BG("tdp_sb", [128, P2], F32)
    tfp_sb = BG("tfp_sb", [128, P2], F32)
    E_sb = BG("E_sb", [128, P2], F32)
    tdC_sb = BG("tdC_sb", [128, P2], F32)
    ramps_sb = BG("ramps_sb", [128, len(RAMP_NAMES) * 128], F32)
    masks_sb = BG("masks_sb", [128, len(MASK_NAMES) * 128], BF16)
    eps_sb = BG("eps_sb", [128, 1], F32)

    def ramp(nm):
        i = RAMP_NAMES.index(nm)
        return ramps_sb.ap[:, 128 * i:128 * (i + 1)]

    def maskv(nm, nh):
        i = MASK_NAMES.index(nm)
        one = masks_sb.ap[:, 128 * i:128 * (i + 1)]
        return one.unsqueeze(1).to_broadcast([128, nh, 128])

    PS = [Buf(nc.alloc_psum_tensor(f"P{i}", [128, 512], F32)[:], f"P{i}")
          for i in range(7)]
    PT = Buf(nc.alloc_psum_tensor("PT", [128, 1024], BF16)[:], "PT")

    for m in mix_in:
        p.dma("y", lambda E, a=mix_sb[m].ap, b=mix_in[m]: E.dma_start(out=a, in_=b),
              writes=(mix_sb[m],))
    p.dma("y", lambda E: E.dma_start(out=tdp_sb.ap, in_=td_in), writes=(tdp_sb,))
    p.dma("y", lambda E: E.dma_start(out=tfp_sb.ap, in_=tf_in), writes=(tfp_sb,))
    p.dma("y", lambda E: E.dma_start(
        out=ramps_sb.ap.rearrange("p (n c) -> p n c", n=len(RAMP_NAMES)),
        in_=ramps_in.unsqueeze(0).to_broadcast(
            [128, len(RAMP_NAMES), 128])), writes=(ramps_sb,))
    p.dma("y", lambda E: E.dma_start(
        out=masks_sb.ap.rearrange("p (n c) -> p n c", n=len(MASK_NAMES)),
        in_=masks_in.transpose([1, 0, 2])), writes=(masks_sb,))
    for m in mix_in:
        p.cop("v", lambda E, o=mix1_sb[m].ap, i=mix_sb[m].ap: E.tensor_scalar(
            o, i, 1.0, -1.0, ALU.subtract, ALU.mult),
            reads=(mix_sb[m],), writes=(mix1_sb[m],))
    p.cop("v", lambda E: E.memset(eps_sb.ap, 1e-5), writes=(eps_sb,))
    p.cop("s", lambda E: E.activation(E_sb.ap, tdp_sb.ap, AF.Exp),
          reads=(tdp_sb,), writes=(E_sb,))
    p.cop("s", lambda E: E.activation(tdC_sb.ap, E_sb.ap, AF.Exp, scale=-float(C)),
          reads=(E_sb,), writes=(tdC_sb,))

    for w in ("wk", "wv", "wr", "wg", "wo") if "cast" in PH else ():
        p.dma("g", lambda E, o=wbf[w].ap, i=wT[w]: E.dma_start(out=o, in_=i),
              writes=wbf[w].rng(0, Dm))

    with contextlib.ExitStack() as ph:
        sbufP, BP = mk_sbuf(ph)
        MH = max(FH, DH)
        NPAR = 4
        dq_pk = [BP(f"dq_pk{i}", [128, MH], I32) for i in range(NPAR)]
        dq_ms = [BP(f"dq_ms{i}", [128, MH], I32) for i in range(NPAR)]
        dq_ls = [BP(f"dq_ls{i}", [128, MH], I32) for i in range(NPAR)]
        dq_sc = [BP(f"dq_sc{i}", [128, 2 * MH], F32) for i in range(2)]
        srow = [BP(f"srow{i}", [1, 4 * MH], BF16) for i in range(2)]
        ind_sb = BP("ind_sb", [1, 256], BF16)
        p.cop("v", lambda E: E.memset(ind_sb.ap, 0.0), writes=(ind_sb,))
        p.cop("v", lambda E: E.memset(ind_sb.ap[0:1, 0:64], 1.0),
              writes=(ind_sb,))
        p.cop("v", lambda E: E.memset(ind_sb.ap[0:1, 192:256], 1.0),
              writes=(ind_sb,))
        dq_om = [BP(f"dq_om{i}", [128, MH], BF16) for i in range(NPAR)]
        dq_ol = [BP(f"dq_ol{i}", [128, MH], BF16) for i in range(NPAR)]
        dq_i = [0]

        def dequant(pk_in, s_in, out_hbm, NQ, W, WHmax):
            nchunks = (W + WHmax - 1) // WHmax
            for qp in range(NQ // 2):
                for hc in range(nchunks):
                    hs = hc * WHmax
                    WH = min(WHmax, W - hs)
                    par = dq_i[0] % NPAR
                    dq_i[0] += 1
                    pkb, scb = dq_pk[par], dq_sc[par % 2]
                    msb_, lsb_ = dq_ms[par], dq_ls[par]
                    omb, olb = dq_om[par], dq_ol[par]
                    pk = pkb.ap[:, 0:WH]
                    for half in range(2):
                        p.dma("g", lambda E, o=pk, i=pk_in, q=qp, h=hs,
                              hf=half, W_=WH: E.dma_start(
                            out=o[64 * hf:64 * hf + 64, :],
                            in_=i[:, 2 * q + hf, h:h + W_]),
                            writes=(pkb,))
                    sc = scb.ap[:, 0:2 * WH].rearrange("p (b c) -> p b c", b=2)
                    srb = srow[par % 2]
                    srv = srb.ap[0:1, 0:4 * WH].rearrange(
                        "p (a b c) -> p a b c", a=2, b=2)
                    p.dma("g", lambda E, o=srv, i=s_in, q=qp, h=hs, W_=WH:
                          E.dma_start(
                        out=o, in_=i[2 * q:2 * q + 2, :, h:h + W_]
                        .unsqueeze(0)), writes=(srb,))
                    nev = 0
                    for b_ in range(2):
                        for cb in range(0, WH, 512):
                            cw = min(512, WH - cb)
                            bank = PS[4 + (nev % 2)]
                            p.cop("t", lambda E, o=bank.ap[:, 0:cw],
                                  a=ind_sb.ap[0:1, 0:128],
                                  m=srv[:, 0, b_, cb:cb + cw]: E.matmul(
                                  o, a, m, start=True, stop=False),
                                  reads=(ind_sb, srb), writes=(bank,))
                            p.cop("t", lambda E, o=bank.ap[:, 0:cw],
                                  a=ind_sb.ap[0:1, 128:256],
                                  m=srv[:, 1, b_, cb:cb + cw]: E.matmul(
                                  o, a, m, start=False, stop=True),
                                  reads=(ind_sb, srb), writes=(bank,),
                                  waw_free=True)
                            eng = "s" if nev % 2 == 0 else "v"
                            if eng == "s":
                                p.cop(eng, lambda E, o=sc[:, b_, cb:cb + cw],
                                      i=bank.ap[:, 0:cw]: E.activation(
                                      o, i, AF.Copy),
                                      reads=(bank,), writes=(scb,))
                            else:
                                p.cop(eng, lambda E, o=sc[:, b_, cb:cb + cw],
                                      i=bank.ap[:, 0:cw]: E.tensor_copy(o, i),
                                      reads=(bank,), writes=(scb,))
                            nev += 1
                    ms, ls = msb_.ap[:, 0:WH], lsb_.ap[:, 0:WH]
                    p.cop("v", lambda E, o=ms, i=pk: E.tensor_single_scalar(
                        o, i, 4, ALU.arith_shift_right),
                        reads=(pkb,), writes=(msb_,))
                    p.cop("v", lambda E, o=ls, i=pk: E.tensor_scalar(
                        o, i, 15, 8, ALU.bitwise_and, ALU.bitwise_xor),
                        reads=(pkb,), writes=(lsb_,))
                    o0, o1 = omb.ap[:, 0:WH], olb.ap[:, 0:WH]
                    p.cop("v", lambda E, o=o0, i=ms, s=sc[:, 0]:
                          E.scalar_tensor_tensor(o, i, 0.0, s, ALU.add,
                                                 ALU.mult),
                          reads=(msb_, scb), writes=(omb,))
                    p.cop("v", lambda E, o=o1, i=ls, s=sc[:, 1]:
                          E.scalar_tensor_tensor(o, i, -8.0, s, ALU.add,
                                                 ALU.mult),
                          reads=(lsb_, scb), writes=(olb,))
                    for half, bsrc in ((0, omb), (1, olb)):
                        for sub in range(2):
                            r0 = 128 * (2 * qp + sub) + 64 * half
                            p.dma("y", lambda E, b=bsrc, r=r0, h=hs, sb_=sub,
                                  W_=WH: E.dma_start(
                                out=out_hbm.ap[r:r + 64, h:h + W_],
                                in_=b.ap[64 * sb_:64 * sb_ + 64, 0:W_]),
                                reads=(bsrc,),
                                writes=out_hbm.rng(r0, r0 + 64, hs, hs + WH))

        if "deq" in PH:
            dequant(w1p, s1d, w1bf, NQ1, F, FH)
            dequant(w3p, s3d, w3bf, NQ1, F, FH)
            dequant(w2p, s2d, w2bf, NQ2, Dm, DH)
        p.barrier()

    inv_d = 1.0 / Dm

    def emit_ln(src_ap, src_row, dst_hbm, dst_row, bufs, src_d=None, n=None):
        xa, st, lst, sq = bufs
        rd = () if src_d is None else tuple(src_d.rng(src_row, src_row + 128))
        p.dma("y", lambda E, o=xa.ap, i=src_ap, t=src_row: E.dma_start(
            out=o, in_=i[t:t + 128, :]), reads=rd, writes=(xa,))
        sums, sumsq = st.ap[:, 0:1], st.ap[:, 1:2]
        mean, varv = st.ap[:, 2:3], st.ap[:, 3:4]
        sd, rstd, m2 = st.ap[:, 4:5], st.ap[:, 5:6], st.ap[:, 6:7]
        p.cop("v", lambda E, o=sums, i=xa.ap: E.reduce_sum(o, i, AX),
              reads=(xa,), writes=(st,))
        p.cop("s", lambda E, o=sq.ap, i=xa.ap, a=sumsq: E.activation(
            o, i, AF.Square, accum_out=a), reads=(xa,), writes=(sq, st))
        p.cop("v", lambda E, o=mean, i=sums: E.tensor_single_scalar(
            o, i, inv_d, ALU.mult), reads=(st,), writes=(st,))
        p.cop("v", lambda E, o=m2, i=mean: E.tensor_tensor(
            o, i, i, ALU.mult), reads=(st,), writes=(st,))
        p.cop("v", lambda E, o=varv, i=sumsq: E.tensor_single_scalar(
            o, i, inv_d, ALU.mult), reads=(st,), writes=(st,))
        p.cop("v", lambda E, o=varv, a=varv, b=m2: E.tensor_sub(o, a, b),
              reads=(st,), writes=(st,))
        p.cop("s", lambda E, o=sd, i=varv: E.activation(
            o, i, AF.Sqrt, bias=eps_sb.ap), reads=(st, eps_sb), writes=(st,))
        p.cop("v", lambda E, o=rstd, i=sd: E.reciprocal(o, i),
              reads=(st,), writes=(st,))
        p.cop("v", lambda E, o=lst.ap, i=xa.ap, a=mean, b=rstd: E.tensor_scalar(
            o, i, a, b, ALU.subtract, ALU.mult), reads=(xa, st), writes=(lst,))
        p.dma("y", lambda E, o=dst_hbm.ap, r=dst_row, i=lst.ap: E.dma_start(
            out=o[r:r + 128, :], in_=i), reads=(lst,),
            writes=dst_hbm.rng(dst_row, dst_row + 128))

    with contextlib.ExitStack() as ph:
        sbufP, BP = mk_sbuf(ph)
        xa_sb = [BP(f"xa{i}", [128, Dm], F32) for i in range(2)]
        sq_sb = [BP(f"sq{i}", [128, Dm], F32) for i in range(2)]
        stat_sb = [BP(f"stt{i}", [128, 8], F32) for i in range(2)]
        ln_st = [BP(f"lnst{i}", [128, Dm], BF16) for i in range(2)]
        p.cop("v", lambda E: E.memset(ln_st[0].ap, 0.0), writes=(ln_st[0],))
        p.dma("y", lambda E: E.dma_start(out=ln1_hbm.ap[0:128, :],
                                         in_=ln_st[0].ap),
              reads=(ln_st[0],), writes=ln1_hbm.rng(0, 128))
        for tt in range(T_all // 128) if "bln" in PH else ():
            par = tt % 2
            emit_ln(x_in, tt * 128, ln1_hbm, 128 + tt * 128,
                    (xa_sb[par], stat_sb[par], ln_st[par], sq_sb[par]))
        p.barrier()

    with contextlib.ExitStack() as ph:
        sbufP, BP = mk_sbuf(ph)
        lnT_t = sbufP("lnT_sb", [128, ND * (TB + 128)], BF16)
        lnT_v = lnT_t[:].rearrange("p (n c) -> p n c", n=ND)
        lnT_sb = [Buf(lnT_v[:, d, :], f"lnT{d}") for d in range(ND)]
        mixT_t = [sbufP(f"mixT{i}", [128, ND * TB], BF16) for i in range(2)]
        mixT_sb = [[Buf(mixT_t[i][:].rearrange("p (n c) -> p n c", n=ND)[:, d, :],
                        f"mixT{i}_{d}") for d in range(ND)] for i in range(2)]
        mixtmp = [BP(f"mixtmp{i}", [128, TB], BF16) for i in range(2)]
        wslab_t = [sbufP(f"wslab{i}", [128, ND * OBW], BF16) for i in range(2)]
        wslab_sb = [Buf(wslab_t[i][:].rearrange("p (n c) -> p n c", n=ND),
                        f"wsl{i}") for i in range(2)]
        ev_sb = [BP(f"ev{i}", [128, OBW], BF16) for i in range(2)]

        def fetch_lnT(t0):
            for d in range(ND):
                p.dma("s", lambda E, o=lnT_sb[d].ap, dd=d, t=t0:
                      E.dma_start_transpose(
                    o, ln1_hbm.ap[t:t + TB + 128, 128 * dd:128 * dd + 128]),
                    reads=tuple(ln1_hbm.rng(t0, t0 + TB + 128)),
                    writes=(lnT_sb[d],))

        def build_mixT(mname, parity):
            mx = mixT_sb[parity]
            for d in range(ND):
                cur = lnT_sb[d].ap[:, 128:128 + TB]
                sh = lnT_sb[d].ap[:, 127:127 + TB]
                tmp = mixtmp[d % 2]
                p.cop("v", lambda E, o=tmp.ap[:, 0:TB], i=sh,
                      s=mix1_sb[mname].ap[:, d:d + 1]: E.tensor_scalar(
                      o, i, s, None, ALU.mult),
                      reads=(lnT_sb[d], mix1_sb[mname]), writes=(tmp,))
                p.cop("v", lambda E, o=mx[d].ap, i=cur,
                      s=mix_sb[mname].ap[:, d:d + 1], t=tmp.ap[:, 0:TB]:
                      E.scalar_tensor_tensor(o, i, s, t, ALU.mult, ALU.add),
                      reads=(lnT_sb[d], mix_sb[mname], tmp), writes=(mx[d],))
            return mx

        def mm_sweep(stat_of, slab, bank, nk, obw, tpos=None):
            for kk in range(nk):
                st_buf, st_ap = stat_of(kk)
                sl_ap = slab.ap[:, kk, 0:obw]
                if kk == 0:
                    p.cop("t", lambda E, o=bank.ap[:, 0:obw], a=st_ap, b=sl_ap:
                          E.matmul(o, a, b, start=True, stop=(nk == 1)),
                        reads=(st_buf, slab), writes=(bank,))
                else:
                    p.cop("t", lambda E, o=bank.ap[:, 0:obw], a=st_ap, b=sl_ap,
                          sp=(kk == nk - 1): E.matmul(o, a, b, start=False,
                                                      stop=sp),
                        reads=(st_buf, slab), writes=(bank,), waw_free=True)

        PROJ = [("wk", "mk", k_hbm, False), ("wv", "mv", v_hbm, False),
                ("wr", "mr", r_hbm, True), ("wg", "mg", g_hbm, True)]
        for blk in range(NB) if "bproj" in PH else ():
            t0 = blk * TB
            fetch_lnT(t0)
            for pi, (w, m, dst, own) in enumerate(PROJ):
                if own and blk < HALO // TB:
                    continue
                mx = build_mixT(m, pi % 2)
                for ob in range(NO):
                    slab = wslab_sb[ob % 2]
                    p.dma("y", lambda E, o=slab.ap[:, :, 0:OBW], i=wbf[w].ap,
                          b=ob: E.dma_start(
                        out=o, in_=i.rearrange("(n p) c -> p n c", p=128)
                        [:, :, OBW * b:OBW * b + OBW]),
                        reads=tuple(wbf[w].rng(0, Dm)), writes=(slab,))
                    for ts in range(NT):
                        bank = PS[ts % 2]
                        mm_sweep(lambda kk: (mx[kk],
                                             mx[kk].ap[:, 128 * ts:128 * ts + 128]),
                                 slab, bank, ND, OBW)
                        evb = ev_sb[ts % 2]
                        fn = AF.Silu if w == "wg" else AF.Copy
                        p.cop("s", lambda E, o=evb.ap[:, 0:OBW],
                              i=bank.ap[:, 0:OBW], f=fn: E.activation(o, i, f),
                              reads=(bank,), writes=(evb,))
                        dst_t = t0 + ts * 128 - (HALO if own else 0)
                        p.dma("y", lambda E, o=dst.ap, t=dst_t, b=ob,
                              i=evb.ap[:, 0:OBW]: E.dma_start(
                            out=o[t:t + 128, OBW * b:OBW * b + OBW], in_=i),
                            reads=(evb,),
                            writes=dst.rng(dst_t, dst_t + 128, ob * OBW,
                                           ob * OBW + OBW))
        p.barrier()

    NMAT = len(RAMP_NAMES)
    KKINDS = ["k4c", "kL4", "k8", "k16", "k32", "k64"]
    RKINDS = ["r4c", "rL4", "r8", "r16", "r32", "r64"]
    NKK = len(KKINDS)
    NRK = len(RKINDS)
    import contextlib as _ctx
    with _ctx.ExitStack() as ph:
        sbufP, BP = mk_sbuf(ph)
        kf = [BP(f"kf{s}", [128, T_all], BF16) for s in range(2)]
        rf = [BP(f"rf{s}", [128, T_own], BF16) for s in range(2)]
        kt = [BP(f"kt{s}", [128, T_all], BF16) for s in range(2)]
        vt = [BP(f"vt{s}", [128, T_all], BF16) for s in range(2)]
        mats = [BP(f"mats{s}", [128, NMAT * 128], BF16) for s in range(2)]
        matz = [[BP(f"matz{s}_{hh}", [128, (NKK + 1) * 128], BF16)
                 for hh in range(2)] for s in range(2)]
        skct = [BP(f"skct{s}", [128, 128], BF16) for s in range(2)]
        matscr = [BP(f"matscr{s}", [128, 128], F32) for s in range(2)]
        idn_sb = BP("idn_sb", [128, 128], BF16)
        hm_sb = BP("hm_sb", [128, 2], F32)
        tfz = [BP(f"tfz{s}", [128, 2], F32) for s in range(2)]
        ksc = [[BP(f"ksc{s}_{hh}", [128, NKK * T_own], BF16) for hh in range(2)]
               for s in range(2)]
        rsc = [BP(f"rsc{s}", [128, NRK * T_own], BF16) for s in range(2)]
        rsiz = [[BP(f"rsiz{s}_{hh}", [128, T_own], BF16) for hh in range(2)]
                for s in range(2)]
        kut = [BP(f"kut{s}", [128, T_all], BF16) for s in range(2)]
        ktfz = [[BP(f"ktfz{s}_{hh}", [128, T_own], BF16) for hh in range(2)]
                for s in range(2)]
        Ssb = [BP(f"S{s}", [128, 64], F32) for s in range(2)]
        Sb16 = [BP(f"Sb{s}", [128, 64], BF16) for s in range(2)]
        stg = [BP(f"stg{i}", [128, 512], BF16) for i in range(5)]
        xnst = BP("xnst", [128, 256], F32)

        def matv(s, nm):
            i = RAMP_NAMES.index(nm)
            return mats[s].ap[:, 128 * i:128 * (i + 1)]

        def matzv(s, hh, nm):
            i = (KKINDS + ["ri"]).index(nm)
            return matz[s][hh].ap[:, 128 * i:128 * (i + 1)]

        p.cop("v", lambda E: E.tensor_copy(
            idn_sb.ap, masks_sb.ap[:, 128 * MASK_NAMES.index("mI"):
                                   128 * (MASK_NAMES.index("mI") + 1)]),
            reads=(masks_sb,), writes=(idn_sb,))
        p.cop("v", lambda E: E.memset(hm_sb.ap, 0.0), writes=(hm_sb,))
        p.cop("v", lambda E: E.memset(hm_sb.ap[0:64, 0:1], 1.0),
              writes=(hm_sb,))
        p.cop("v", lambda E: E.memset(hm_sb.ap[64:128, 1:2], 1.0),
              writes=(hm_sb,))

        psA, psB_, psC, psD, psE = PS[0], PS[1], PS[2], PS[3], PS[4]
        psO, psU = PS[5], PS[6]

        for grp in range(NG) if "wkv" in PH else ():
            for s in range(2):
                pr = 2 * grp + s
                p.dma("s", lambda E, o=kf[s].ap, q=pr: E.dma_start_transpose(
                    o, k_hbm.ap[:, 128 * q:128 * q + 128]),
                    reads=tuple(k_hbm.rng(0, T_all, 128 * pr, 128 * pr + 128)),
                    writes=(kf[s],))
                p.dma("s", lambda E, o=rf[s].ap, q=pr: E.dma_start_transpose(
                    o, r_hbm.ap[:, 128 * q:128 * q + 128]),
                    reads=tuple(r_hbm.rng(0, T_own, 128 * pr, 128 * pr + 128)),
                    writes=(rf[s],))
                p.dma("y", lambda E, o=kt[s].ap.rearrange(
                    "p (n c) -> p n c", n=NCH), q=pr: E.dma_start(
                    out=o, in_=k_hbm.ap.rearrange("(n p) c -> p n c", p=128)
                    [:, :, 128 * q:128 * q + 128]),
                    reads=tuple(k_hbm.rng(0, T_all, 128 * pr, 128 * pr + 128)),
                    writes=(kt[s],))
                p.dma("y", lambda E, o=vt[s].ap.rearrange(
                    "p (n c) -> p n c", n=NCH), q=pr: E.dma_start(
                    out=o, in_=v_hbm.ap.rearrange("(n p) c -> p n c", p=128)
                    [:, :, 128 * q:128 * q + 128]),
                    reads=tuple(v_hbm.rng(0, T_all, 128 * pr, 128 * pr + 128)),
                    writes=(vt[s],))
                for nm in RAMP_NAMES:
                    p.cop("v", lambda E, o=matscr[s].ap, r=ramp(nm),
                          e=E_sb.ap[:, pr:pr + 1]: E.tensor_scalar(
                          o, r, e, None, ALU.mult),
                          reads=(ramps_sb, E_sb), writes=(matscr[s],))
                    p.cop("s", lambda E, o=matv(s, nm), i=matscr[s].ap:
                          E.activation(o, i, AF.Exp, scale=-1.0),
                          reads=(matscr[s],), writes=(mats[s],))
                for hh in range(2):
                    for nm in KKINDS + ["ri"]:
                        p.cop("v", lambda E, o=matzv(s, hh, nm),
                              i=matv(s, nm), m=hm_sb.ap[:, hh:hh + 1]:
                              E.tensor_scalar(o, i, m, None, ALU.mult),
                              reads=(mats[s], hm_sb), writes=(matz[s][hh],))
                p.cop("v", lambda E, o=tfz[s].ap,
                      i=tfp_sb.ap[:, pr:pr + 1].to_broadcast([128, 2]),
                      m=hm_sb.ap: E.tensor_tensor(o, i, m, ALU.mult),
                      reads=(tfp_sb, hm_sb), writes=(tfz[s],))
                p.cop("t", lambda E, o=PT.ap[:, 0:128], i=matv(s, "ku"):
                      E.transpose(o, i, idn_sb.ap),
                      reads=(mats[s], idn_sb), writes=(PT,))
                p.cop("v", lambda E, o=skct[s].ap, i=PT.ap[:, 0:128]:
                      E.tensor_copy(o, i), reads=(PT,), writes=(skct[s],))
                nto = T_own // 128
                for hh in range(2):
                    for ki in range(NKK):
                        nm = KKINDS[ki]
                        p.cop("g", lambda E,
                              o=ksc[s][hh].ap[:, ki * T_own:(ki + 1) * T_own]
                              .rearrange("p (n c) -> p n c", c=128),
                              i=kf[s].ap[:, HALO:T_all]
                              .rearrange("p (n c) -> p n c", c=128),
                              m=matzv(s, hh, nm).unsqueeze(1)
                              .to_broadcast([128, nto, 128]):
                              E.tensor_tensor(o, i, m, ALU.mult),
                              reads=(kf[s], matz[s][hh]), writes=(ksc[s][hh],))
                    p.cop("g", lambda E,
                          o=rsiz[s][hh].ap.rearrange("p (n c) -> p n c", c=128),
                          i=rf[s].ap.rearrange("p (n c) -> p n c", c=128),
                          m=matzv(s, hh, "ri").unsqueeze(1)
                          .to_broadcast([128, nto, 128]):
                          E.tensor_tensor(o, i, m, ALU.mult),
                          reads=(rf[s], matz[s][hh]), writes=(rsiz[s][hh],))
                    p.cop("v", lambda E, o=ktfz[s][hh].ap,
                          i=kf[s].ap[:, HALO:T_all],
                          t=tfz[s].ap[:, hh:hh + 1]: E.tensor_scalar(
                          o, i, t, None, ALU.mult),
                          reads=(kf[s], tfz[s]), writes=(ktfz[s][hh],))
                for ri_ in range(NRK):
                    nm = RKINDS[ri_]
                    p.cop("g", lambda E,
                          o=rsc[s].ap[:, ri_ * T_own:(ri_ + 1) * T_own]
                          .rearrange("p (n c) -> p n c", c=128),
                          i=rf[s].ap.rearrange("p (n c) -> p n c", c=128),
                          m=matv(s, nm).unsqueeze(1)
                          .to_broadcast([128, nto, 128]):
                          E.tensor_tensor(o, i, m, ALU.mult),
                          reads=(rf[s], mats[s]), writes=(rsc[s],))
                p.cop("g", lambda E,
                      o=kut[s].ap.rearrange("p (n c) -> p n c", c=128),
                      i=kt[s].ap.rearrange("p (n c) -> p n c", c=128),
                      m=skct[s].ap.unsqueeze(1).to_broadcast([128, NCH, 128]):
                      E.tensor_tensor(o, i, m, ALU.mult),
                      reads=(kt[s], skct[s]), writes=(kut[s],))
                p.cop("v", lambda E, o=Ssb[s].ap: E.memset(o, 0.0),
                      writes=(Ssb[s],))
                p.cop("v", lambda E, o=Sb16[s].ap: E.memset(o, 0.0),
                      writes=(Sb16[s],))

            def ksz(s, hh, nm, j0, jn):
                ki = KKINDS.index(nm)
                return ksc[s][hh].ap[:, ki * T_own + j0:ki * T_own + j0 + jn]

            def rsp(s, nm, i0, in_):
                ri_ = RKINDS.index(nm)
                return rsc[s].ap[:, ri_ * T_own + i0:ri_ * T_own + i0 + in_]

            for ch in (range(NCH) if "wkv2" in PH else ()):
                tok0 = ch * C
                owned = ch >= NCH_H and "wkv3" in PH
                oc0 = tok0 - HALO
                if owned:
                    p.cop("v", lambda E: E.memset(psE.ap, 0.0), writes=(psE,))
                    args = {psA.name: [], psB_.name: [], psC.name: [],
                            psD.name: [], psE.name: []}
                    bankof = {b.name: b for b in (psA, psB_, psC, psD, psE)}
                    def amm(bank, stat_ap, mov_ap, rbufs, region, bp=0):
                        args[bank.name].append((stat_ap, mov_ap, rbufs, region,
                                                bp))
                    for s in range(2):
                        for hh in range(2):
                            q = 2 * s + hh
                            qc = slice(128 * q, 128 * q + 128)
                            kb = ksc[s][hh]
                            vb = (kb, rsc[s])
                            amm(psA, ksz(s, hh, "k4c", oc0, 128),
                                rsp(s, "r4c", oc0, 128), vb, psA.ap[:, qc])
                            amm(psB_, ksz(s, hh, "kL4", oc0, 128),
                                rsp(s, "rL4", oc0, 128), vb, psB_.ap[:, qc])
                            amm(psC, ksz(s, hh, "k8", oc0, 128),
                                rsp(s, "r8", oc0, 128), vb, psC.ap[:, qc])
                            amm(psD, ktfz[s][hh].ap[:, oc0:oc0 + 128],
                                rf[s].ap[:, oc0:oc0 + 128],
                                (ktfz[s][hh], rf[s]), psD.ap[:, qc])
                            amm(psE, ksz(s, hh, "k64", oc0, 64),
                                rsp(s, "r64", oc0 + 64, 64), vb,
                                psE.ap[0:64, 128 * q + 64:128 * q + 128])
                            for wloc in range(2):
                                j0 = 64 * wloc
                                amm(psE, ksz(s, hh, "k32", oc0 + j0, 32),
                                    rsp(s, "r32", oc0 + j0 + 32, 32), vb,
                                    psE.ap[j0:j0 + 32,
                                           128 * q + j0 + 32:128 * q + j0 + 64],
                                    j0)
                            for wloc in range(4):
                                j0 = 32 * wloc
                                amm(psE, ksz(s, hh, "k16", oc0 + j0, 16),
                                    rsp(s, "r16", oc0 + j0 + 16, 16), vb,
                                    psE.ap[j0:j0 + 16,
                                           128 * q + j0 + 16:128 * q + j0 + 32],
                                    j0)
                    for bname, mms in args.items():
                        bank = bankof[bname]
                        ise = bname == psE.name
                        if ise and "wkvAE" not in PH:
                            continue
                        if not ise and "wkvA4" not in PH:
                            continue
                        for mi, (stat_ap, mov_ap, rbufs, region, bp) in \
                                enumerate(mms):
                            st = (mi == 0) and not ise
                            sp = (mi == len(mms) - 1) and not ise
                            tp = (0, bp) if bp else None
                            p.cop("t", lambda E, o=region, a=stat_ap, b=mov_ap,
                                  st_=st, sp_=sp, sk=ise, tp_=tp: E.matmul(
                                  o, a, b, start=st_, stop=sp_,
                                  skip_group_check=sk, tile_position=tp_),
                                  reads=rbufs, writes=(bank,),
                                  waw_free=(mi != 0))
                    evlist = []
                    if "wkvA4" in PH and "wkvEv" in PH:
                        evlist += [(0, psA, "m4"), (1, psB_, "m8"),
                                   (2, psC, "m16"), (3, psD, "mI")]
                    if "wkvAE" in PH and "wkvEv" in PH:
                        evlist += [(4, psE, "mB")]
                    for stgi, bank, mnm in evlist:
                        p.cop("v", lambda E,
                              o=stg[stgi].ap.rearrange("p (n c) -> p n c", c=128),
                              i=bank.ap.rearrange("p (n c) -> p n c", c=128),
                              m=maskv(mnm, 4): E.tensor_tensor(
                              o, i, m, ALU.mult),
                              reads=(bank, masks_sb), writes=(stg[stgi],))
                    omms = [] if "wkv4" in PH else None
                    for s in (range(2) if omms is not None else ()):
                        for hh in range(2):
                            q = 2 * s + hh
                            oreg = psO.ap[:, 64 * q:64 * q + 64]
                            vmov = vt[s].ap[:, tok0 + 64 * hh:tok0 + 64 * hh + 64]
                            for stgi in range(5):
                                omms.append((stg[stgi].ap[:, 128 * q:128 * q + 128],
                                             vmov, (stg[stgi], vt[s]), oreg))
                            omms.append((rsiz[s][hh].ap[:, oc0:oc0 + 128],
                                         Sb16[s].ap,
                                         (rsiz[s][hh], Sb16[s]), oreg))
                    for mi, (stat_ap, mov_ap, rbufs, region) in \
                            enumerate(omms or ()):
                        p.cop("t", lambda E, o=region, a=stat_ap, b=mov_ap,
                              st_=(mi == 0), sp_=(mi == len(omms) - 1):
                              E.matmul(o, a, b, start=st_, stop=sp_),
                              reads=rbufs, writes=(psO,), waw_free=(mi != 0))
                p.cop("v", lambda E: E.memset(psU.ap[:, 0:128], 0.0),
                      writes=(psU,))
                for s in range(2):
                    for hh in range(2):
                        h0 = 64 * hh
                        p.cop("t", lambda E,
                              o=psU.ap[h0:h0 + 64, 64 * s:64 * s + 64],
                              a=kut[s].ap[:, tok0 + h0:tok0 + h0 + 64],
                              b=vt[s].ap[:, tok0 + h0:tok0 + h0 + 64],
                              tp=(0, h0): E.matmul(
                              o, a, b, start=False, stop=False,
                              tile_position=tp, skip_group_check=True),
                              reads=(kut[s], vt[s]), writes=(psU,),
                              waw_free=not (s == 0 and hh == 0))
                for s in range(2):
                    pr = 2 * grp + s
                    p.cop("v", lambda E, o=Ssb[s].ap, i=Ssb[s].ap,
                          c=tdC_sb.ap[:, pr:pr + 1],
                          u=psU.ap[:, 64 * s:64 * s + 64]:
                          E.scalar_tensor_tensor(o, i, c, u, ALU.mult, ALU.add),
                          reads=(Ssb[s], tdC_sb, psU), writes=(Ssb[s],))
                    p.cop("v", lambda E, o=Sb16[s].ap, i=Ssb[s].ap:
                          E.tensor_copy(o, i), reads=(Ssb[s],), writes=(Sb16[s],))
                if owned and "wkv4" in PH:
                    p.cop("s", lambda E, o=xnst.ap, i=psO.ap[:, 0:256]:
                          E.activation(o, i, AF.Copy), reads=(psO,),
                          writes=(xnst,))
                    for s in range(2):
                        pr = 2 * grp + s
                        p.dma("y", lambda E, o=xn_hbm.ap, r=oc0, q=pr,
                              i=xnst.ap[:, 128 * s:128 * s + 128]: E.dma_start(
                            out=o[r:r + 128, 128 * q:128 * q + 128], in_=i),
                            reads=(xnst,),
                            writes=xn_hbm.rng(oc0, oc0 + 128, 128 * pr,
                                              128 * pr + 128))
        p.barrier()

    inv64 = 1.0 / 64.0
    with _ctx.ExitStack() as ph:
        sbufP, BP = mk_sbuf(ph)
        xng = [BP(f"xng{i}", [128, Dm], F32) for i in range(2)]
        gg = [BP(f"gg{i}", [128, Dm], BF16) for i in range(2)]
        sqg = BP("sqg", [128, Dm], F32)
        stg2 = [BP(f"stg2{i}", [128, 8 * H], F32) for i in range(2)]
        yst = [BP(f"yst{i}", [128, Dm], BF16) for i in range(2)]
        gnw_bc = BP("gnw_bc", [128, Dm], F32)
        gnb_bc = BP("gnb_bc", [128, Dm], F32)
        p.dma("y", lambda E: E.dma_start(
            out=gnw_bc.ap, in_=gnw_in.to_broadcast([128, Dm])),
            writes=(gnw_bc,))
        p.dma("y", lambda E: E.dma_start(
            out=gnb_bc.ap, in_=gnb_in.to_broadcast([128, Dm])),
            writes=(gnb_bc,))
        for tt in range(T_own // 128) if "gn" in PH else ():
            par = tt % 2
            xa, st, yb, gb = xng[par], stg2[par], yst[par], gg[par]
            t0 = tt * 128
            p.dma("y", lambda E, o=xa.ap, t=t0: E.dma_start(
                out=o, in_=xn_hbm.ap[t:t + 128, :]),
                reads=tuple(xn_hbm.rng(t0, t0 + 128)), writes=(xa,))
            p.dma("y", lambda E, o=gb.ap, t=t0: E.dma_start(
                out=o, in_=g_hbm.ap[t:t + 128, :]),
                reads=tuple(g_hbm.rng(t0, t0 + 128)), writes=(gb,))
            sums = st.ap[:, 0:H]
            sumsq = st.ap[:, H:2 * H]
            mean = st.ap[:, 2 * H:3 * H]
            varv = st.ap[:, 3 * H:4 * H]
            sd = st.ap[:, 4 * H:5 * H]
            rstd = st.ap[:, 5 * H:6 * H]
            m2 = st.ap[:, 6 * H:7 * H]
            xv = xa.ap.rearrange("p (h e) -> p h e", h=H)
            p.cop("v", lambda E, o=sums, i=xv: E.reduce_sum(o, i, AX),
                  reads=(xa,), writes=(st,))
            p.cop("s", lambda E, o=sqg.ap, i=xa.ap: E.activation(
                o, i, AF.Square), reads=(xa,), writes=(sqg,))
            p.cop("v", lambda E, o=sumsq,
                  i=sqg.ap.rearrange("p (h e) -> p h e", h=H):
                  E.reduce_sum(o, i, AX), reads=(sqg,), writes=(st,))
            p.cop("v", lambda E, o=mean, i=sums: E.tensor_single_scalar(
                o, i, inv64, ALU.mult), reads=(st,), writes=(st,))
            p.cop("v", lambda E, o=m2, i=mean: E.tensor_tensor(
                o, i, i, ALU.mult), reads=(st,), writes=(st,))
            p.cop("v", lambda E, o=varv, i=sumsq: E.tensor_single_scalar(
                o, i, inv64, ALU.mult), reads=(st,), writes=(st,))
            p.cop("v", lambda E, o=varv, a=varv, b=m2: E.tensor_sub(o, a, b),
                  reads=(st,), writes=(st,))
            p.cop("s", lambda E, o=sd, i=varv: E.activation(
                o, i, AF.Sqrt, bias=eps_sb.ap), reads=(st, eps_sb), writes=(st,))
            p.cop("v", lambda E, o=rstd, i=sd: E.reciprocal(o, i),
                  reads=(st,), writes=(st,))
            meanb = mean.unsqueeze(2).to_broadcast([128, H, 64])
            rstdb = rstd.unsqueeze(2).to_broadcast([128, H, 64])
            p.cop("g", lambda E, o=xv, i=xv, m=meanb: E.tensor_sub(o, i, m),
                  reads=(xa, st), writes=(xa,))
            p.cop("v", lambda E, o=xv, i=xv, m=rstdb: E.tensor_tensor(
                o, i, m, ALU.mult), reads=(xa, st), writes=(xa,))
            p.cop("g", lambda E, o=xa.ap, i=xa.ap, m=gnw_bc.ap: E.tensor_tensor(
                o, i, m, ALU.mult), reads=(xa, gnw_bc), writes=(xa,))
            p.cop("g", lambda E, o=xa.ap, i=xa.ap, m=gnb_bc.ap: E.tensor_add(
                o, i, m), reads=(xa, gnb_bc), writes=(xa,))
            p.cop("v", lambda E, o=yb.ap, i=xa.ap, m=gb.ap: E.tensor_tensor(
                o, i, m, ALU.mult), reads=(xa, gb), writes=(yb,))
            p.dma("y", lambda E, o=y_hbm.ap, t=t0, i=yb.ap: E.dma_start(
                out=o[t:t + 128, :], in_=i), reads=(yb,),
                writes=y_hbm.rng(t0, t0 + 128))
        p.barrier()

    with _ctx.ExitStack() as ph:
        sbufP, BP = mk_sbuf(ph)
        yT_t = sbufP("yT_sb", [128, ND * T_own], BF16)
        yT_v = yT_t[:].rearrange("p (n c) -> p n c", n=ND)
        yT_sb = [Buf(yT_v[:, d, :], f"yT{d}") for d in range(ND)]
        woslab_t = [sbufP(f"wosl{i}", [128, ND * OBW], BF16) for i in range(2)]
        woslab = [Buf(woslab_t[i][:].rearrange("p (n c) -> p n c", n=ND),
                      f"wosl{i}") for i in range(2)]
        xp = [BP(f"xp{i}", [128, OBW], F32) for i in range(2)]
        x2st = [BP(f"x2st{i}", [128, OBW], F32) for i in range(2)]
        for d in range(ND) if "op" in PH else ():
            p.dma("s", lambda E, o=yT_sb[d].ap, dd=d: E.dma_start_transpose(
                o, y_hbm.ap[:, 128 * dd:128 * dd + 128]),
                reads=tuple(y_hbm.rng(0, T_own)), writes=(yT_sb[d],))
        for ob in range(NO) if "op" in PH else ():
            slab = woslab[ob % 2]
            p.dma("y", lambda E, o=slab.ap[:, :, 0:OBW], b=ob: E.dma_start(
                out=o, in_=wbf["wo"].ap.rearrange("(n p) c -> p n c", p=128)
                [:, :, OBW * b:OBW * b + OBW]),
                reads=tuple(wbf["wo"].rng(0, Dm)), writes=(slab,))
            for ts in range(T_own // 128):
                bank = PS[ts % 2]
                for kk in range(ND):
                    p.cop("t", lambda E, o=bank.ap[:, 0:OBW],
                          a=yT_sb[kk].ap[:, 128 * ts:128 * ts + 128],
                          b=slab.ap[:, kk, 0:OBW], st=(kk == 0),
                          sp=(kk == ND - 1):
                          E.matmul(o, a, b, start=st, stop=sp),
                          reads=(yT_sb[kk], slab), writes=(bank,),
                          waw_free=(kk != 0))
                xpb = xp[ts % 2]
                x2b = x2st[ts % 2]
                p.dma("y", lambda E, o=xpb.ap, t=ts * 128, b=ob: E.dma_start(
                    out=o, in_=x_in[HALO + t:HALO + t + 128,
                                    OBW * b:OBW * b + OBW]),
                    writes=(xpb,))
                p.cop("v", lambda E, o=x2b.ap, i=bank.ap[:, 0:OBW], m=xpb.ap:
                      E.tensor_add(o, i, m), reads=(bank, xpb), writes=(x2b,))
                p.dma("y", lambda E, o=x2_hbm.ap, t=ts * 128, b=ob, i=x2b.ap:
                      E.dma_start(out=o[t:t + 128, OBW * b:OBW * b + OBW],
                                  in_=i), reads=(x2b,),
                      writes=x2_hbm.rng(ts * 128, ts * 128 + 128, ob * OBW,
                                        ob * OBW + OBW))
        p.barrier()

    with _ctx.ExitStack() as ph:
        sbufP, BP = mk_sbuf(ph)
        xa_sb = [BP(f"xb{i}", [128, Dm], F32) for i in range(2)]
        sq_sb = [BP(f"sqb{i}", [128, Dm], F32) for i in range(2)]
        stat_sb = [BP(f"stb{i}", [128, 8], F32) for i in range(2)]
        ln_st = [BP(f"lnb{i}", [128, Dm], BF16) for i in range(2)]
        for tt in range(T_own // 128) if "ln2" in PH else ():
            par = tt % 2
            emit_ln(x2_hbm.ap, tt * 128, ln2_hbm, tt * 128,
                    (xa_sb[par], stat_sb[par], ln_st[par], sq_sb[par]),
                    src_d=x2_hbm)
        p.barrier()

    NHB = (F + 511) // 512
    with _ctx.ExitStack() as ph:
        sbufP, BP = mk_sbuf(ph)
        ln2T_t = sbufP("ln2T_sb", [128, ND * TB2], BF16)
        ln2T_v = ln2T_t[:].rearrange("p (n c) -> p n c", n=ND)
        ln2T_sb = [Buf(ln2T_v[:, d, :], f"l2T{d}") for d in range(ND)]
        w1sl_t = [sbufP(f"w1sl{i}", [128, ND * 512], BF16) for i in range(2)]
        w1sl = [Buf(w1sl_t[i][:].rearrange("p (n c) -> p n c", n=ND), f"w1sl{i}")
                for i in range(2)]
        w3sl_t = [sbufP(f"w3sl{i}", [128, ND * 512], BF16) for i in range(2)]
        w3sl = [Buf(w3sl_t[i][:].rearrange("p (n c) -> p n c", n=ND), f"w3sl{i}")
                for i in range(2)]
        sast = [BP(f"sast{i}", [128, 512], F32) for i in range(2)]
        hst = [BP(f"hst{i}", [128, 512], BF16) for i in range(2)]
        for fb in range(NFB) if "ffa" in PH else ():
            ft0 = fb * TB2
            for d in range(ND):
                p.dma("s", lambda E, o=ln2T_sb[d].ap, dd=d, t=ft0:
                      E.dma_start_transpose(
                    o, ln2_hbm.ap[t:t + TB2, 128 * dd:128 * dd + 128]),
                    reads=tuple(ln2_hbm.rng(ft0, ft0 + TB2)),
                    writes=(ln2T_sb[d],))
            for hb in range(NHB):
                hw = min(512, F - 512 * hb)
                sl1, sl3 = w1sl[hb % 2], w3sl[hb % 2]
                p.dma("y", lambda E, o=sl1.ap[:, :, 0:hw], b=hb, w_=hw:
                      E.dma_start(
                    out=o, in_=w1bf.ap.rearrange("(n p) c -> p n c", p=128)
                    [:, :, 512 * b:512 * b + w_]),
                    reads=tuple(w1bf.rng(0, Dm, 512 * hb, 512 * hb + hw)),
                    writes=(sl1,))
                p.dma("y", lambda E, o=sl3.ap[:, :, 0:hw], b=hb, w_=hw:
                      E.dma_start(
                    out=o, in_=w3bf.ap.rearrange("(n p) c -> p n c", p=128)
                    [:, :, 512 * b:512 * b + w_]),
                    reads=tuple(w3bf.rng(0, Dm, 512 * hb, 512 * hb + hw)),
                    writes=(sl3,))
                for ts in range(TB2 // 128):
                    bka = PS[2 * (ts % 2)]
                    bkb = PS[2 * (ts % 2) + 1]
                    for kk in range(ND):
                        p.cop("t", lambda E, o=bka.ap[:, 0:hw],
                              a=ln2T_sb[kk].ap[:, 128 * ts:128 * ts + 128],
                              b=sl1.ap[:, kk, 0:hw], st=(kk == 0),
                              sp=(kk == ND - 1):
                              E.matmul(o, a, b, start=st, stop=sp),
                              reads=(ln2T_sb[kk], sl1), writes=(bka,),
                              waw_free=(kk != 0))
                    for kk in range(ND):
                        p.cop("t", lambda E, o=bkb.ap[:, 0:hw],
                              a=ln2T_sb[kk].ap[:, 128 * ts:128 * ts + 128],
                              b=sl3.ap[:, kk, 0:hw], st=(kk == 0),
                              sp=(kk == ND - 1):
                              E.matmul(o, a, b, start=st, stop=sp),
                              reads=(ln2T_sb[kk], sl3), writes=(bkb,),
                              waw_free=(kk != 0))
                    sab = sast[ts % 2]
                    hb_st = hst[ts % 2]
                    p.cop("s", lambda E, o=sab.ap[:, 0:hw], i=bka.ap[:, 0:hw]:
                          E.activation(o, i, AF.Silu), reads=(bka,),
                          writes=(sab,))
                    p.cop("v", lambda E, o=hb_st.ap[:, 0:hw],
                          i=sab.ap[:, 0:hw], m=bkb.ap[:, 0:hw]:
                          E.tensor_tensor(o, i, m, ALU.mult),
                          reads=(sab, bkb), writes=(hb_st,))
                    p.dma("y", lambda E, o=h_hbm.ap, t=ft0 + ts * 128, b=hb,
                          w_=hw, i=hb_st.ap: E.dma_start(
                        out=o[t:t + 128, 512 * b:512 * b + w_],
                        in_=i[:, 0:w_]), reads=(hb_st,),
                        writes=h_hbm.rng(ft0 + ts * 128, ft0 + ts * 128 + 128,
                                         512 * hb, 512 * hb + hw))
        p.barrier()

    with _ctx.ExitStack() as ph:
        sbufP, BP = mk_sbuf(ph)
        hT_t = sbufP("hT_sb", [128, NQ2 * TB2], BF16)
        hT_v = hT_t[:].rearrange("p (n c) -> p n c", n=NQ2)
        hT_sb = [Buf(hT_v[:, d, :], f"hT{d}") for d in range(NQ2)]
        w2sl_t = [sbufP(f"w2sl{i}", [128, NQ2 * OB2], BF16) for i in range(2)]
        w2sl = [Buf(w2sl_t[i][:].rearrange("p (n c) -> p n c", n=NQ2),
                    f"w2sl{i}") for i in range(2)]
        xp2 = [BP(f"xp2{i}", [128, OB2], F32) for i in range(2)]
        outst = [BP(f"outst{i}", [128, OB2], F32) for i in range(2)]
        for fb in range(NFB) if "ffw2" in PH else ():
            ft0 = fb * TB2
            for d in range(NQ2):
                p.dma("s", lambda E, o=hT_sb[d].ap, dd=d, t=ft0:
                      E.dma_start_transpose(
                    o, h_hbm.ap[t:t + TB2, 128 * dd:128 * dd + 128]),
                    reads=tuple(h_hbm.rng(ft0, ft0 + TB2, 128 * d,
                                          128 * d + 128)),
                    writes=(hT_sb[d],))
            for ob in range(NO2):
                slab = w2sl[ob % 2]
                p.dma("y", lambda E, o=slab.ap[:, :, 0:OB2], b=ob: E.dma_start(
                    out=o, in_=w2bf.ap.rearrange("(n p) c -> p n c", p=128)
                    [:, :, OB2 * b:OB2 * b + OB2]),
                    reads=tuple(w2bf.rng(0, F)), writes=(slab,))
                for ts in range(TB2 // 128):
                    bank = PS[ts % 2]
                    for kk in range(NQ2):
                        p.cop("t", lambda E, o=bank.ap[:, 0:OB2],
                              a=hT_sb[kk].ap[:, 128 * ts:128 * ts + 128],
                              b=slab.ap[:, kk, 0:OB2], st=(kk == 0),
                              sp=(kk == NQ2 - 1):
                              E.matmul(o, a, b, start=st, stop=sp),
                              reads=(hT_sb[kk], slab), writes=(bank,),
                              waw_free=(kk != 0))
                    xpb = xp2[ts % 2]
                    osb = outst[ts % 2]
                    t0 = ft0 + ts * 128
                    p.dma("y", lambda E, o=xpb.ap, t=t0, b=ob: E.dma_start(
                        out=o, in_=x2_hbm.ap[t:t + 128, OB2 * b:OB2 * b + OB2]),
                        reads=tuple(x2_hbm.rng(t0, t0 + 128, ob * OB2,
                                               ob * OB2 + OB2)),
                        writes=(xpb,))
                    p.cop("v", lambda E, o=osb.ap, i=bank.ap[:, 0:OB2],
                          m=xpb.ap: E.tensor_add(o, i, m),
                          reads=(bank, xpb), writes=(osb,))
                    p.dma("y", lambda E, o=out_c, t=t0, b=ob, i=osb.ap:
                          E.dma_start(out=o[t:t + 128, OB2 * b:OB2 * b + OB2],
                                      in_=i), reads=(osb,))
        p.barrier()

    p.finish("y")

    stack = emit_all(nc, p)
    return nc



def host_prepare(cfg, inputs):
    n_cores = cfg["n_cores"]
    Dm, F, T_own, HALO = cfg["Dm"], cfg["F"], cfg["T_own"], cfg["HALO"]
    T_all = T_own + HALO
    ND = Dm // 128
    P2 = Dm // 128
    NQ1 = Dm // 128
    NQ2 = F // 128

    x = np.asarray(inputs["L0_attn_hidden"], np.float32).reshape(-1, Dm)
    base = {}
    for nm, w in (("wkT", "L0_attn_w_key"), ("wvT", "L0_attn_w_value"),
                  ("wrT", "L0_attn_w_receptance"), ("wgT", "L0_attn_w_gate"),
                  ("woT", "L0_attn_w_output")):
        base[nm] = np.ascontiguousarray(np.asarray(inputs[w], np.float32).T)
    for nm, m in (("mk", "L0_attn_time_mix_key"), ("mv", "L0_attn_time_mix_value"),
                  ("mr", "L0_attn_time_mix_receptance"),
                  ("mg", "L0_attn_time_mix_gate")):
        base[nm] = np.ascontiguousarray(
            np.asarray(inputs[m], np.float32).reshape(ND, 128).T)
    td = np.asarray(inputs["L0_attn_time_decay"], np.float32)
    tf = np.asarray(inputs["L0_attn_time_faaaa"], np.float32)
    base["td_pair"] = np.ascontiguousarray(td.reshape(P2, 128).T)
    base["tf_pair"] = np.ascontiguousarray(tf.reshape(P2, 128).T)
    base["gnw"] = np.asarray(inputs["L0_attn_gn_weight"], np.float32).reshape(1, Dm)
    base["gnb"] = np.asarray(inputs["L0_attn_gn_bias"], np.float32).reshape(1, Dm)
    base["ramps"] = make_ramps()
    base["masks"] = make_masks()
    for nm, wid, nq in (("1", F, NQ1), ("3", F, NQ1), ("2", Dm, NQ2)):
        w = np.asarray(inputs["L0_ffn_w" + nm], np.int32)
        s = np.asarray(inputs["L0_ffn_s" + nm], np.float32)
        base["w" + nm + "p"] = np.ascontiguousarray(
            w.reshape(wid, nq, 64).transpose(2, 1, 0))
        base["s" + nm + "d"] = np.ascontiguousarray(
            s.reshape(wid, nq, 2).transpose(1, 2, 0))

    in_maps = []
    for c in range(n_cores):
        m = dict(base)
        xc = np.zeros((T_all, Dm), np.float32)
        start = c * T_own - HALO
        lo = max(0, -start)
        xc[lo:] = x[start + lo: c * T_own + T_own]
        m["x_c"] = xc
        in_maps.append(m)
    return in_maps


def kernel(**inputs):
    cfg = full_cfg()
    nc = build_program(cfg)
    in_maps = host_prepare(cfg, inputs)
    res = run_bass_kernel_spmd(nc, in_maps, core_ids=list(range(cfg["n_cores"])))
    outs = [np.asarray(r["out_c"], np.float32) for r in res.results]
    full = np.concatenate(outs, axis=0)
    return full.reshape(1, cfg["n_cores"] * cfg["T_own"], cfg["Dm"])
